# version 17
# baseline (speedup 1.0000x reference)
import contextlib
import numpy as np
import ml_dtypes
import concourse.bass as bass
import concourse.mybir as mybir
from concourse.bass_utils import run_bass_kernel_spmd

F32 = mybir.dt.float32
BF16 = mybir.dt.bfloat16
I32 = mybir.dt.int32
AF = mybir.ActivationFunctionType
ALU = mybir.AluOpType

DM = 1024
DSSM = 2048
NH = 32
PD = 64
CONVD = 2304
DIP = 4384
HID = 3072
CH = 256
EPS = 1e-5
QSC = 1.0 / 127.0

TCORE = 2048
ST = 512
NSUP = TCORE // ST
FCA = CONVD // 128
FCZ = DSSM // 128
FCS = DM // 128
FCH = HID // 128
NFIN = DM // 128
NCK = TCORE // CH
W3 = ST + 3


DEBUG = False


def build_program():
    nc = bass.Bass("TRN2", target_bir_lowering=False)

    x_in = nc.dram_tensor("x", [128 + TCORE, DM], F32, kind="ExternalInput").ap()
    w_in = nc.dram_tensor("w_in", [DIP, DM], F32, kind="ExternalInput").ap()
    w_out = nc.dram_tensor("w_out", [DM, DSSM], F32, kind="ExternalInput").ap()
    wg_i = nc.dram_tensor("wg", [HID, DM], I32, kind="ExternalInput").ap()
    wu_i = nc.dram_tensor("wu", [HID, DM], I32, kind="ExternalInput").ap()
    wd_i = nc.dram_tensor("wd", [DM, HID], I32, kind="ExternalInput").ap()
    convdiag = nc.dram_tensor("convdiag", [FCA, 128, 512], F32, kind="ExternalInput").ap()
    convb_fm = nc.dram_tensor("convb_fm", [128, FCA], F32, kind="ExternalInput").ap()
    dtbias = nc.dram_tensor("dtbias", [NH, 1], F32, kind="ExternalInput").ap()
    alog = nc.dram_tensor("alog", [NH, 1], F32, kind="ExternalInput").ap()
    dexp_fm = nc.dram_tensor("dexp_fm", [128, FCZ], F32, kind="ExternalInput").ap()
    normw_fm = nc.dram_tensor("normw_fm", [128, FCZ], F32, kind="ExternalInput").ap()
    ident_in = nc.dram_tensor("ident", [128, 128], F32, kind="ExternalInput").ap()
    identbf_in = nc.dram_tensor("identbf", [128, 128], BF16, kind="ExternalInput").ap()
    trimlt_in = nc.dram_tensor("trimlt", [128, 2 * CH], BF16, kind="ExternalInput").ap()
    rep_in = nc.dram_tensor("rep", [NH, DSSM], BF16, kind="ExternalInput").ap()
    ones3_in = nc.dram_tensor("ones3", [3, 128], BF16, kind="ExternalInput").ap()
    sel_in = nc.dram_tensor("sel", [128, 4], F32, kind="ExternalInput").ap()

    out_d = nc.dram_tensor("out", [TCORE, DM], F32, kind="ExternalOutput").ap()
    if DEBUG:
        dbg_xbc = nc.dram_tensor("dbg_xbc", [FCA, 128, ST], F32, kind="ExternalOutput").ap()
        dbg_xnh = nc.dram_tensor("dbg_xnh", [NFIN, 128, ST], F32, kind="ExternalOutput").ap()
        dbg_dtp = nc.dram_tensor("dbg_dtp", [NH, TCORE], F32, kind="ExternalOutput").ap()
        dbg_sta = nc.dram_tensor("dbg_sta", [128, DSSM], F32, kind="ExternalOutput").ap()
        dbg_spref = nc.dram_tensor("dbg_spref", [128, DSSM], F32, kind="ExternalOutput").ap()
        dbg_carry = nc.dram_tensor("dbg_carry", [128, DSSM], F32, kind="ExternalOutput").ap()
        dbg_y = nc.dram_tensor("dbg_y", [128, FCZ * ST], F32, kind="ExternalOutput").ap()
        dbg_t = nc.dram_tensor("dbg_t", [128, FCZ * ST], F32, kind="ExternalOutput").ap()
        dbg_s = nc.dram_tensor("dbg_s", [128, FCS * ST], F32, kind="ExternalOutput").ap()
        dbg_g = nc.dram_tensor("dbg_g", [128, FCS * ST], F32, kind="ExternalOutput").ap()
        dbg_x4t = nc.dram_tensor("dbg_x4t", [128, DSSM], F32, kind="ExternalOutput").ap()
        dbg_wpl = nc.dram_tensor("dbg_wpl", [128, CONVD + 32], F32, kind="ExternalOutput").ap()
        dbg_raw = nc.dram_tensor("dbg_raw", [FCA, 128, ST + 3], F32, kind="ExternalOutput").ap()
        dbg_em = nc.dram_tensor("dbg_em", [128, 512], F32, kind="ExternalOutput").ap()
        dbg_mn = nc.dram_tensor("dbg_mn", [128, 512], F32, kind="ExternalOutput").ap()
        dbg_sta7 = nc.dram_tensor("dbg_sta7", [128, DSSM], F32, kind="ExternalOutput").ap()
        dbg_em1 = nc.dram_tensor("dbg_em1", [128, 512], F32, kind="ExternalOutput").ap()
        dbg_ctk = nc.dram_tensor("dbg_ctk", [128, 2 * NH], F32, kind="ExternalOutput").ap()
        dbg_y3 = nc.dram_tensor("dbg_y3", [128, 512], F32, kind="ExternalOutput").ap()
        dbg_yn = nc.dram_tensor("dbg_yn", [128, FCZ * ST], F32, kind="ExternalOutput").ap()
        dbg_s1 = nc.dram_tensor("dbg_s1", [128, FCS * ST], F32, kind="ExternalOutput").ap()
        dbg_g1 = nc.dram_tensor("dbg_g1", [128, FCS * ST], F32, kind="ExternalOutput").ap()
        dbg_t1 = nc.dram_tensor("dbg_t1", [128, FCZ * ST], F32, kind="ExternalOutput").ap()
        dbg_ln = nc.dram_tensor("dbg_ln", [128, 8], F32, kind="ExternalOutput").ap()
        dbg_xnt = nc.dram_tensor("dbg_xnt", [128, DM], F32, kind="ExternalOutput").ap()

    scrA_hi = nc.dram_tensor("scrA_hi", [NFIN, 128, CONVD + 32], BF16, kind="Internal").ap()
    scrA_lo = nc.dram_tensor("scrA_lo", [NFIN, 128, CONVD + 32], BF16, kind="Internal").ap()
    scrZ_hi = nc.dram_tensor("scrZ_hi", [NFIN, 128, DSSM], BF16, kind="Internal").ap()
    scrZ_lo = nc.dram_tensor("scrZ_lo", [NFIN, 128, DSSM], BF16, kind="Internal").ap()
    scrO_hi = nc.dram_tensor("scrO_hi", [FCZ, 128, DM], BF16, kind="Internal").ap()
    scrO_lo = nc.dram_tensor("scrO_lo", [FCZ, 128, DM], BF16, kind="Internal").ap()
    scrG = nc.dram_tensor("scrG", [NFIN, 128, HID], BF16, kind="Internal").ap()
    scrU = nc.dram_tensor("scrU", [NFIN, 128, HID], BF16, kind="Internal").ap()
    scrD = nc.dram_tensor("scrD", [FCH, 128, DM], BF16, kind="Internal").ap()
    scr_xnh = nc.dram_tensor("scr_xnh", [NSUP, NFIN, 128, ST], BF16, kind="Internal").ap()
    scr_xnl = nc.dram_tensor("scr_xnl", [NSUP, NFIN, 128, ST], BF16, kind="Internal").ap()
    scr_xbc = nc.dram_tensor("scr_xbc", [NSUP, FCA, 128, ST], F32, kind="Internal").ap()
    scr_x4t = nc.dram_tensor("scr_x4t", [NCK, 2, 128, DSSM], F32, kind="Internal").ap()
    scr_sta = nc.dram_tensor("scr_sta", [NCK, 128, DSSM], F32, kind="Internal").ap()
    cc_in = nc.dram_tensor("cc_in", [128, 2056], F32, kind="Internal").ap()
    cc_out = nc.dram_tensor("cc_out", [4, 128, 2056], F32, kind="Internal",
                            addr_space="Local").ap()

    es = contextlib.ExitStack()

    def sbt(name, shape, dt):
        return es.enter_context(nc.sbuf_tensor(name, shape, dt))

    c_id = sbt("c_id", [128, 128], F32)
    c_idbf = sbt("c_idbf", [128, 128], BF16)
    c_trimlt = sbt("c_trimlt", [128, 2 * CH], BF16)
    c_rep = sbt("c_rep", [NH, DSSM], BF16)
    c_ones3 = sbt("c_ones3", [3, 128], BF16)
    c_convb = sbt("c_convb", [128, FCA], F32)
    c_dexp = sbt("c_dexp", [128, FCZ], F32)
    c_normw = sbt("c_normw", [128, FCZ], F32)
    c_sel = sbt("c_sel", [128, 4], F32)
    c_dtb = sbt("c_dtb", [NH, 1], F32)
    c_alog = sbt("c_alog", [NH, 1], F32)
    p_A = sbt("p_A", [NH, 1], F32)
    ones32 = sbt("ones32", [NH, CH], F32)
    onescol = sbt("onescol", [128, 1], F32)

    a_xn = sbt("a_xn", [128, 4096], F32)
    a_xbc = sbt("a_xbc", [128, FCA * W3], F32)
    a_y = sbt("a_y", [128, FCZ * ST], F32)
    a_big = sbt("a_big", [128, 8192], F32)
    a_carry = sbt("a_carry", [128, DSSM], F32)
    a_wrk = sbt("a_wrk", [128, 4672], F32)

    s_dtpf = sbt("s_dtpf", [NH, TCORE], F32)
    s_csk = sbt("s_csk", [NH, CH], F32)
    s_sp1 = sbt("s_sp1", [NH, ST], F32)
    s_sp2 = sbt("s_sp2", [NH, ST], F32)
    s_dtr = sbt("s_dtr", [NH, ST], F32)
    s_hdtp = sbt("s_hdtp", [NH, 128], BF16)
    s_ldtp = sbt("s_ldtp", [NH, 128], BF16)
    s_csk1 = sbt("s_csk1", [NH, CH], F32)
    s_csk2 = sbt("s_csk2", [NH, CH], F32)
    s_cshi = sbt("s_cshi", [NH, CH], BF16)
    s_cslo = sbt("s_cslo", [NH, CH], BF16)
    s_csll = sbt("s_csll", [NH, CH], BF16)
    csflat3 = sbt("csflat3", [3, NH * CH], BF16)
    s_decl = sbt("s_decl", [NH, CH], F32)
    s_q = sbt("s_q", [NH, CH], F32)
    s_qh = sbt("s_qh", [NH, 128], BF16)
    s_ql = sbt("s_ql", [NH, 128], BF16)
    s_cd = sbt("s_cd", [NH, 1], F32)
    s_cdw = sbt("s_cdw", [NH, 128], F32)
    s_cdwh = sbt("s_cdwh", [NH, 128], BF16)
    s_cdwl = sbt("s_cdwl", [NH, 128], BF16)
    s_pp = sbt("s_pp", [NH, 1], F32)
    s_pj = sbt("s_pj", [NH, 4], F32)
    s_bmt = sbt("s_bmt", [128, 2 * 128], F32)
    s_ln = sbt("s_ln", [128, 8], F32)
    s_rms0 = sbt("s_rms0", [1, ST], F32)
    s_rms1 = sbt("s_rms1", [1, ST], F32)
    s_ctok = sbt("s_ctok", [128, 2 * NH], F32)
    s_enb = sbt("s_enb", [128, 512], F32)
    s_em0 = sbt("s_em0", [128, 512], F32)
    s_em1 = sbt("s_em1", [128, 512], F32)
    s_mnb = sbt("s_mnb", [128, 512], F32)
    s_halo = sbt("s_halo", [128, 3 * FCA], F32)
    prep_t = sbt("prep_t", [128, 2048], BF16)

    pb = [es.enter_context(nc.psum_tensor(f"pb{i}", [128, 512], F32)) for i in range(8)]

    class Sem:
        def __init__(self, name, queue=None):
            self.h = es.enter_context(nc.semaphore(name))
            self.n = 0
            self.queue = queue
            if queue is not None:
                self.ch = es.enter_context(nc.semaphore(name + "c"))
                self.cn = 0

    sWLa = Sem("sWLa", "sync")
    sWLb = Sem("sWLb", "sync")
    sIO = Sem("sIO", "sync")
    sGP = Sem("sGP", "gpsimd")
    sPE = Sem("sPE")
    sACT = Sem("sACT")
    sDVE = Sem("sDVE")
    sST = Sem("sST", "sync")
    sCC = Sem("sCC")
    sTR = Sem("sTR", "sync")
    sIO2 = Sem("sIO2", "scalar")
    sTR2 = Sem("sTR2", "scalar")

    plan = {k: [] for k in ["sync", "gpsimd", "scalar", "vector", "tensor"]}

    def em(eng, f):
        plan[eng].append(f)

    def wait(eng, sem, tgt):
        if tgt and tgt > 0:
            em(eng, lambda e, s=sem.h, t=tgt: e.wait_ge(s, t))

    def dma(q, out_ap, in_ap, sem, slow=False):
        sem.n += 16
        em(q, lambda e, o=out_ap, i=in_ap, s=sem.h, sl=slow: e.dma_start(
            out=o, in_=i, allow_slow_non_contiguous=sl).then_inc(s, 16))
        return sem.n

    def dmat(q, out_ap, in_ap, sem):
        sem.n += 16
        em(q, lambda e, o=out_ap, i=in_ap, s=sem.h: e.dma_start_transpose(o, i).then_inc(s, 16))
        return sem.n

    def settle(sem):
        wait(sem.queue, sem, sem.n)
        sem.cn += 1
        em(sem.queue, lambda e, s=sem.ch: e.nop().then_inc(s, 1))
        return sem.cn

    def dwait(eng, sem):
        t = settle(sem)
        em(eng, lambda e, s=sem.ch, t=t: e.wait_ge(s, t))

    def op(eng, fn, sem=None):
        if sem is not None:
            sem.n += 1
            em(eng, lambda e, f=fn, s=sem.h: f(e).then_inc(s, 1))
            return sem.n
        em(eng, lambda e, f=fn: f(e))
        return None

    def dbg(name_ap, src_ap):
        if DEBUG:
            dma("sync", name_ap, src_ap, sST)

    xn_hi = a_xn.ap()[:, 0:2048].bitcast(BF16)
    xn_lo = a_xn.ap()[:, 2048:4096].bitcast(BF16)
    hxn_hi = a_big.ap()[:, 6656:7168].bitcast(BF16)
    hxn_lo = a_big.ap()[:, 7168:7680].bitcast(BF16)

    def xbc_t(fc):
        return a_xbc.ap()[:, fc * W3:(fc + 1) * W3]

    yn_hi = a_xbc.ap()[:, 0:4096].bitcast(BF16)
    yn_lo = a_xbc.ap()[:, 4096:8192].bitcast(BF16)

    def ybuf(fc):
        return a_y.ap()[:, fc * ST:(fc + 1) * ST]

    def sb8(fc):
        return a_y.ap()[:, fc * ST:(fc + 1) * ST]

    def gacc(do):
        return a_y.ap()[:, (FCS + do) * ST:(FCS + do + 1) * ST]

    x4t = [a_big.ap()[:, 0:2048], a_big.ap()[:, 2048:4096]]
    stg = a_big.ap()[:, 4096:6144]
    xddb = a_big.ap()[:, 6144:6656]
    ffn_h32 = a_big.ap()[:, 6656:7168]
    prep_ffn = a_big.ap()[:, 6656:8192].bitcast(BF16)
    prep_ffn2 = a_big.ap()[:, 4096:5632].bitcast(BF16)
    gath = [a_big.ap()[:, j * 2048:(j + 1) * 2048] for j in range(4)]
    hh = a_big.ap()[:, 0:3072].bitcast(BF16)
    hl = a_big.ap()[:, 3072:6144].bitcast(BF16)

    w_xtok = a_wrk.ap()[:, 0:1024]
    w_dump = a_wrk.ap()[:, 1024:2048]
    w_xntok = a_wrk.ap()[:, 2048:3072]
    tokhi = a_wrk.ap()[:, 3072:3584].bitcast(BF16)
    toklo = a_wrk.ap()[:, 3584:4096].bitcast(BF16)
    prep_nat = a_wrk.ap()[:, 0:1024]
    prep_hi = a_wrk.ap()[:, 1024:1536].bitcast(BF16)
    prep_lo = a_wrk.ap()[:, 1536:2048].bitcast(BF16)
    prep_nat2 = a_wrk.ap()[:, 2048:3072]
    prep_hi2 = a_wrk.ap()[:, 3072:3584].bitcast(BF16)
    prep_lo2 = a_wrk.ap()[:, 3584:4096].bitcast(BF16)
    w_ostage = a_wrk.ap()[:, 0:1024]
    w_xre = a_wrk.ap()[:, 1024:2048]
    cdg2 = [a_wrk.ap()[:, 0:512], a_wrk.ap()[:, 512:1024]]
    wbh = [a_wrk.ap()[:, 0:1168].bitcast(BF16), a_wrk.ap()[:, 1168:2336].bitcast(BF16)]
    wbl = [a_wrk.ap()[:, 2336:3504].bitcast(BF16), a_wrk.ap()[:, 3504:4672].bitcast(BF16)]
    fbg = a_wrk.ap()[:, 0:1536].bitcast(BF16)
    fbu = a_wrk.ap()[:, 1536:3072].bitcast(BF16)

    for dst, src in [(c_id.ap(), ident_in), (c_idbf.ap(), identbf_in),
                     (c_trimlt.ap(), trimlt_in), (c_rep.ap(), rep_in),
                     (c_ones3.ap(), ones3_in), (c_convb.ap(), convb_fm),
                     (c_dexp.ap(), dexp_fm), (c_normw.ap(), normw_fm),
                     (c_sel.ap(), sel_in), (c_dtb.ap(), dtbias)]:
        dma("sync", dst, src, sIO)
    t_const = dma("sync", c_alog.ap(), alog, sIO)

    wait("vector", sIO, t_const)
    op("vector", lambda e: e.memset(ones32[:], 1.0))
    op("vector", lambda e: e.memset(onescol[:], 1.0))
    op("vector", lambda e: e.memset(a_carry.ap(), 0.0))
    op("vector", lambda e: e.memset(s_pp[:], 1.0), sDVE)
    wait("scalar", sIO, t_const)
    op("scalar", lambda e: e.activation(p_A[:], c_alog[:], AF.Exp))
    t_A = op("scalar", lambda e: e.activation(p_A[:], p_A[:], AF.Copy, scale=-1.0), sACT)

    PIPE_A = dict(q="sync", sio=sIO, str_=sTR,
                  nat=prep_nat, hi=prep_hi, lo=prep_lo, pt0=0)
    PIPE_B = dict(q="scalar", sio=sIO2, str_=sTR2,
                  nat=prep_nat2, hi=prep_hi2, lo=prep_lo2, pt0=8)

    def prep_tiles(P, pairs):
        assert len(pairs) <= 8
        q, sio, strq = P["q"], P["sio"], P["str_"]
        wait(q, sio, sio.n)
        for t, (s_ap, _, rr) in enumerate(pairs):
            dmat(q, prep_t[:, (P["pt0"] + t) * 128:(P["pt0"] + t) * 128 + rr], s_ap, strq)
        wait(q, strq, strq.n)
        for t, (_, dst, rr) in enumerate(pairs):
            dma(q, dst, prep_t[:, (P["pt0"] + t) * 128:(P["pt0"] + t) * 128 + rr], sio)

    def prep_f32_rows(P, src_rows, rr, width, dst_hi, dst_lo, fo_off):
        q, sio, strq = P["q"], P["sio"], P["str_"]
        nat, hi, lo = P["nat"], P["hi"], P["lo"]
        wait(q, sACT, P.get("act_t", 0))
        wait(q, sDVE, P.get("dve_t", 0))
        dma(q, nat[0:rr, 0:width], src_rows, sio)
        dwait("scalar", sio)
        dwait("scalar", strq)
        t_h = op("scalar", lambda e, r=rr, w=width, hi=hi, nat=nat: e.activation(
            hi[0:r, 0:w], nat[0:r, 0:w], AF.Copy), sACT)
        wait("vector", sACT, t_h)
        dwait("vector", strq)
        t_lo = op("vector", lambda e, r=rr, w=width, hi=hi, lo=lo, nat=nat: e.tensor_tensor(
            lo[0:r, 0:w], nat[0:r, 0:w], hi[0:r, 0:w], ALU.subtract), sDVE)
        P["act_t"] = t_h
        P["dve_t"] = t_lo
        wait(q, sACT, t_h)
        wait(q, sDVE, t_lo)
        prep_tiles(P, [(hi[0:rr, cc * 128:(cc + 1) * 128],
                        dst_hi[P.get("po", 0) + cc, :, fo_off:fo_off + rr], rr)
                       for cc in range(width // 128)])
        prep_tiles(P, [(lo[0:rr, cc * 128:(cc + 1) * 128],
                        dst_lo[P.get("po", 0) + cc, :, fo_off:fo_off + rr], rr)
                       for cc in range(width // 128)])

    for blk in range(FCA):
        prep_f32_rows(PIPE_A if blk % 2 == 0 else PIPE_B,
                      w_in[2048 + blk * 128:2048 + (blk + 1) * 128, :], 128, 1024,
                      scrA_hi, scrA_lo, blk * 128)
    prep_f32_rows(PIPE_A, w_in[4352:4384, :], 32, 1024, scrA_hi, scrA_lo, CONVD)
    t_prepA = sIO.n

    PIPE_I = dict(q="sync", sio=sIO, str_=sTR,
                  nat=a_y.ap()[:, 0:1024],
                  hi=a_y.ap()[:, 1024:1536].bitcast(BF16),
                  lo=a_y.ap()[:, 1536:2048].bitcast(BF16), pt0=0)
    prep_jobs = []
    for blk in range(FCZ):
        prep_jobs.append(lambda blk=blk: prep_f32_rows(
            PIPE_I, w_in[blk * 128:(blk + 1) * 128, :], 128, 1024, scrZ_hi, scrZ_lo, blk * 128))
    for blk in range(FCS):
        for half in range(2):
            def _j(blk=blk, half=half):
                PIPE_I["po"] = half * 8
                prep_f32_rows(PIPE_I, w_out[blk * 128:(blk + 1) * 128, half * 1024:(half + 1) * 1024],
                              128, 1024, scrO_hi, scrO_lo, blk * 128)
                PIPE_I["po"] = 0
            prep_jobs.append(_j)

    def prep_int_rows(P, src_rows, width, dst, fo_off):
        buf = prep_ffn
        dwait("gpsimd", P["str_"])
        dma("gpsimd", buf[:, 0:width], src_rows, sGP)
        dwait(P["q"], sGP)
        for g0 in range(0, width // 128, 8):
            prep_tiles(P, [(buf[:, cc * 128:(cc + 1) * 128], dst[cc, :, fo_off:fo_off + 128], 128)
                           for cc in range(g0, min(g0 + 8, width // 128))])

    for blk in range(FCH):
        prep_jobs.append(lambda blk=blk: prep_int_rows(
            PIPE_I, wg_i[blk * 128:(blk + 1) * 128, :], 1024, scrG, blk * 128))
        prep_jobs.append(lambda blk=blk: prep_int_rows(
            PIPE_I, wu_i[blk * 128:(blk + 1) * 128, :], 1024, scrU, blk * 128))
    for blk in range(FCS):
        prep_jobs.append(lambda blk=blk: prep_int_rows(
            PIPE_I, wd_i[blk * 128:(blk + 1) * 128, :], 3072, scrD, blk * 128))
    prep_jobs.reverse()

    def run_prep_jobs(n):
        for _ in range(n):
            if prep_jobs:
                prep_jobs.pop()()
    t_prep_all = sIO.n
    t_prep_gp = sGP.n

    def ln_group(x_rows_ap, dst_hi_fm, dst_lo_fm, col_off, fm_w, dump=False):
        wait("sync", sPE, sPE.n)
        wait("sync", sACT, sACT.n)
        wait("sync", sDVE, sDVE.n)
        dma("sync", w_xtok, x_rows_ap, sIO)
        dwait("scalar", sIO)
        wait("scalar", sDVE, sDVE.n)
        op("scalar", lambda e: e.activation(w_dump, w_xtok, AF.Copy, accum_out=s_ln.ap()[:, 0:1]))
        t_s = op("scalar", lambda e: e.activation(w_dump, w_xtok, AF.Square, accum_out=s_ln.ap()[:, 1:2]), sACT)
        wait("vector", sACT, t_s)
        op("vector", lambda e: e.tensor_scalar_mul(s_ln.ap()[:, 2:3], s_ln.ap()[:, 0:1], 1.0 / DM))
        op("vector", lambda e: e.tensor_scalar_mul(s_ln.ap()[:, 3:4], s_ln.ap()[:, 1:2], 1.0 / DM))
        t_v = op("vector", lambda e: e.scalar_tensor_tensor(
            s_ln.ap()[:, 4:5], s_ln.ap()[:, 2:3], s_ln.ap()[:, 2:3], s_ln.ap()[:, 3:4],
            ALU.mult, ALU.subtract), sDVE)
        wait("scalar", sDVE, t_v)
        t_sd = op("scalar", lambda e: e.activation(s_ln.ap()[:, 5:6], s_ln.ap()[:, 4:5],
                                                   AF.Sqrt, bias=np.float32(EPS), scale=-1.0), sACT)
        wait("vector", sACT, t_sd)
        op("vector", lambda e: e.reciprocal(s_ln.ap()[:, 6:7], s_ln.ap()[:, 5:6]))
        t_xn = op("vector", lambda e: e.tensor_scalar(
            w_xntok, w_xtok, s_ln.ap()[:, 2:3], s_ln.ap()[:, 6:7],
            ALU.subtract, ALU.mult), sDVE)
        wait("scalar", sDVE, t_xn)
        dwait("scalar", sTR)
        t_h = op("scalar", lambda e: e.activation(tokhi, w_xntok, AF.Copy), sACT)
        wait("vector", sACT, t_h)
        dwait("vector", sTR)
        t_l = op("vector", lambda e: e.tensor_tensor(toklo, w_xntok, tokhi, ALU.subtract), sDVE)
        wait("sync", sACT, t_h)
        wait("sync", sDVE, t_l)
        if DEBUG and dump:
            dma("sync", dbg_ln, s_ln.ap(), sST)
            dma("sync", dbg_xnt, w_xntok, sST)
        for fc in range(NFIN):
            dmat("sync", dst_hi_fm[:, fc * fm_w + col_off:fc * fm_w + col_off + 128],
                 tokhi[:, fc * 128:(fc + 1) * 128], sTR)
            dmat("sync", dst_lo_fm[:, fc * fm_w + col_off:fc * fm_w + col_off + 128],
                 toklo[:, fc * 128:(fc + 1) * 128], sTR)
        return sTR.n

    def proj(fo_slices, psum_list, planes_hi, planes_lo, mov_hi_fn, mov_lo_fn,
             n_fin, two_pass=False, fin_base=0):
        lo_off = min(s[0] for s in fo_slices)
        hi_off = max(s[0] + s[1] for s in fo_slices)
        w = hi_off - lo_off
        wait("tensor", sACT, sACT.n)
        wait("tensor", sDVE, sDVE.n)
        wait("sync", sPE, sPE.n)
        wait("sync", sTR, sTR.n)
        pe_after = {}
        for fin in range(n_fin):
            b = fin % 2
            if fin >= 2:
                wait("sync", sPE, pe_after[fin - 2])
            s_w = sWLa if b == 0 else sWLb
            dma("sync", wbh[b][:, 0:w], planes_hi[fin_base + fin, :, lo_off:hi_off], s_w)
            if not two_pass:
                dma("sync", wbl[b][:, 0:w], planes_lo[fin_base + fin, :, lo_off:hi_off], s_w)
            wait("tensor", s_w, s_w.n)
            mh = mov_hi_fn(fin)
            ml = mov_lo_fn(fin)
            mw = mh.shape[-1]
            for j, (fo, fw) in enumerate(fo_slices):
                o = fo - lo_off
                stt = (fin == 0)
                stp = (fin == n_fin - 1)
                em("tensor", lambda e, j=j, b=b, o=o, fw=fw, s=stt, mh=mh, mw=mw: e.matmul(
                    psum_list[j][0:fw, 0:mw], wbh[b][:, o:o + fw], mh, start=s, stop=False))
                if two_pass:
                    op("tensor", lambda e, j=j, b=b, o=o, fw=fw, ml=ml, mw=mw, p=stp: e.matmul(
                        psum_list[j][0:fw, 0:mw], wbh[b][:, o:o + fw], ml, start=False, stop=p), sPE)
                else:
                    em("tensor", lambda e, j=j, b=b, o=o, fw=fw, ml=ml, mw=mw: e.matmul(
                        psum_list[j][0:fw, 0:mw], wbh[b][:, o:o + fw], ml, start=False, stop=False))
                    op("tensor", lambda e, j=j, b=b, o=o, fw=fw, mh=mh, mw=mw, p=stp: e.matmul(
                        psum_list[j][0:fw, 0:mw], wbl[b][:, o:o + fw], mh, start=False, stop=p), sPE)
            pe_after[fin] = sPE.n
        return sPE.n

    def hilo(src, dst_h, dst_l):
        wait("scalar", sDVE, sDVE.n)
        th = op("scalar", lambda e: e.activation(dst_h, src, AF.Copy), sACT)
        wait("vector", sACT, th)
        tl = op("vector", lambda e: e.tensor_tensor(dst_l, src, dst_h, ALU.subtract), sDVE)
        return th, tl

    wait("sync", sIO, t_prep_all)
    dwait("sync", sIO2)
    dwait("sync", sGP)
    if DEBUG:
        dma("sync", wbh[0][:, 0:2336], scrA_lo[0], sWLa)
        dwait("gpsimd", sWLa)
        sGP.n += 16
        em("gpsimd", lambda e, s=sGP.h: e.dma_start(
            out=dbg_wpl, in_=wbh[0][:, 0:2336]).then_inc(s, 16))
        wait("sync", sGP, sGP.n)
    ln_group(x_in[0:128, :], hxn_hi, hxn_lo, 0, 128)
    dwait("tensor", sTR)
    wait("tensor", sACT, t_A)
    for grp in range(3):
        fos = [(f * 128, 128) for f in range(grp * 6, grp * 6 + 6)]
        t_pe = proj(fos, [pb[j] for j in range(6)], scrA_hi, scrA_lo,
                    lambda fin: hxn_hi[:, fin * 128:(fin + 1) * 128],
                    lambda fin: hxn_lo[:, fin * 128:(fin + 1) * 128], NFIN)
        wait("scalar", sPE, t_pe)
        for j in range(6):
            fc = grp * 6 + j
            op("scalar", lambda e, j=j, fc=fc: e.activation(
                xbc_t(fc)[:, 0:3], pb[j][0:128, 125:128], AF.Copy), sACT)
        wait("tensor", sACT, sACT.n)

    for st in range(NSUP):
        for tg in range(4):
            t_tr = ln_group(x_in[128 + st * ST + tg * 128: 128 + st * ST + (tg + 1) * 128, :],
                            xn_hi, xn_lo, tg * 128, ST, dump=(st == 0 and tg == 0))
        wait("sync", sTR, sTR.n)
        for f in range(NFIN):
            dma("sync", scr_xnh[st, f], xn_hi[:, f * ST:(f + 1) * ST], sIO)
            dma("sync", scr_xnl[st, f], xn_lo[:, f * ST:(f + 1) * ST], sIO)
        if DEBUG and st == 0:
            dwait("gpsimd", sTR)
            for f in range(NFIN):
                sGP.n += 16
                em("gpsimd", lambda e, f=f, s=sGP.h: e.dma_start(
                    out=dbg_xnh[f], in_=xn_hi[:, f * ST:(f + 1) * ST]).then_inc(s, 16))

        dwait("tensor", sTR)
        dwait("scalar", sIO)
        groups = [[(f * 128, 128) for f in range(0, 8)],
                  [(f * 128, 128) for f in range(8, 16)],
                  [(16 * 128, 128), (17 * 128, 128), (CONVD, 32)]]
        for gi, fos in enumerate(groups):
            t_pe = proj(fos, [pb[j] for j in range(len(fos))], scrA_hi, scrA_lo,
                        lambda fin: xn_hi[:, fin * ST:(fin + 1) * ST],
                        lambda fin: xn_lo[:, fin * ST:(fin + 1) * ST], NFIN)
            wait("scalar", sPE, t_pe)
            wait("vector", sPE, t_pe)
            nfc = 8 if gi < 2 else 2
            for j in range(nfc):
                fc = gi * 8 + j
                if j % 2 == 0:
                    op("scalar", lambda e, j=j, fc=fc: e.activation(
                        xbc_t(fc)[:, 3:3 + ST], pb[j][:, 0:ST], AF.Copy), sACT)
                else:
                    op("vector", lambda e, j=j, fc=fc: e.tensor_copy(
                        xbc_t(fc)[:, 3:3 + ST], pb[j][:, 0:ST]), sDVE)
            if gi == 2:
                wait("vector", sACT, sACT.n)
                op("vector", lambda e: e.tensor_copy(s_dtr[:, :], pb[2][0:NH, 0:ST]), sDVE)
            wait("tensor", sACT, sACT.n)
            wait("tensor", sDVE, sDVE.n)

        t_evac = sACT.n
        if DEBUG and st == 0:
            wait("sync", sACT, sACT.n)
            wait("sync", sDVE, sDVE.n)
            for fc in range(FCA):
                dbg(dbg_raw[fc], xbc_t(fc)[:, 0:ST + 3])
        wait("sync", sPE, sPE.n)
        conv_silu_t = [sACT.n, sACT.n]
        dma("sync", cdg2[0], convdiag[0], sWLa)
        for fc in range(FCA):
            b = fc % 2
            if st > 0:
                wait("vector", sACT, t_evac)
                op("vector", lambda e, fc=fc: e.tensor_copy(
                    xbc_t(fc)[:, 0:3], s_halo.ap()[:, fc * 3:fc * 3 + 3]), sDVE)
            wait("vector", sACT, t_evac)
            t_tail = op("vector", lambda e, fc=fc: e.tensor_copy(
                s_halo.ap()[:, fc * 3:fc * 3 + 3], xbc_t(fc)[:, ST:ST + 3]), sDVE)
            s_w = sWLa if b == 0 else sWLb
            t_wl = s_w.n
            if fc + 1 < FCA:
                wait("sync", sPE, sPE.n)
                dma("sync", cdg2[(fc + 1) % 2], convdiag[fc + 1],
                    sWLb if b == 0 else sWLa)
            wait("tensor", s_w, t_wl)
            wait("tensor", sDVE, t_tail)
            cb = 6 + (fc % 2)
            wait("tensor", sACT, conv_silu_t[fc % 2])
            for k in range(4):
                op("tensor", lambda e, fc=fc, k=k, b=b, cb=cb: e.matmul(
                    pb[cb][:, 0:ST], cdg2[b][:, k * 128:(k + 1) * 128],
                    xbc_t(fc)[:, k:k + ST], start=(k == 0), stop=(k == 3)),
                    sPE if k == 3 else None)
            t_mm = sPE.n
            wait("scalar", sPE, t_mm)
            op("scalar", lambda e, fc=fc, cb=cb: e.activation(
                xbc_t(fc)[:, 3:3 + ST], pb[cb][:, 0:ST], AF.Silu,
                bias=c_convb.ap()[:, fc:fc + 1]), sACT)
            conv_silu_t[fc % 2] = sACT.n
        t_conv = sACT.n
        wait("sync", sACT, t_conv)
        for fc in range(FCA):
            dma("sync", scr_xbc[st, fc], xbc_t(fc)[:, 3:3 + ST], sIO)
        if DEBUG and st == 0:
            for fc in range(FCA):
                dbg(dbg_xbc[fc], xbc_t(fc)[:, 3:3 + ST])

        wait("scalar", sDVE, sDVE.n)
        op("scalar", lambda e: e.activation(s_sp1[:, :], s_dtr[:, :], AF.Abs,
                                            bias=c_dtb.ap()[:, 0:1]))
        op("scalar", lambda e: e.activation(s_sp1[:, :], s_sp1[:, :], AF.Exp, scale=-1.0))
        op("scalar", lambda e: e.activation(s_sp1[:, :], s_sp1[:, :], AF.Ln, bias=np.float32(1.0)))
        t_sp = op("scalar", lambda e: e.activation(s_sp2[:, :], s_dtr[:, :], AF.Relu,
                                                   bias=c_dtb.ap()[:, 0:1]), sACT)
        wait("vector", sACT, t_sp)
        op("vector", lambda e, st=st: e.tensor_tensor(
            s_dtpf[:, st * ST:(st + 1) * ST], s_sp1[:, :], s_sp2[:, :], ALU.add), sDVE)
        op("vector", lambda e, st=st: e.tensor_scalar_mul(
            s_sp1[:, :], s_dtpf[:, st * ST:(st + 1) * ST], p_A.ap()[:, 0:1]), sDVE)

        for c in range(2):
            ck = st * 2 + c
            csk = s_csk.ap()[:, :]
            wait("vector", sACT, sACT.n)
            op("vector", lambda e, c=c: e.tensor_tensor_scan(
                s_csk[:, :], ones32[:, :], s_sp1[:, c * CH:(c + 1) * CH],
                0.0, ALU.mult, ALU.add), sDVE)
            t_cs = sDVE.n
            wait("scalar", sDVE, t_cs)
            op("scalar", lambda e, csk=csk: e.activation(s_cd[:, :], csk[:, CH - 1:CH], AF.Exp))
            t_decl = op("scalar", lambda e, csk=csk: e.activation(
                s_decl[:, :], csk, AF.Exp, bias=csk[:, CH - 1:CH], scale=-1.0), sACT)
            wait("vector", sACT, t_decl)
            op("vector", lambda e, ck=ck, c=c: e.tensor_tensor(
                s_q[:, :], s_dtpf[:, ck * CH:(ck + 1) * CH], s_decl[:, :], ALU.mult))
            wait("vector", sPE, sPE.n)
            op("vector", lambda e: e.tensor_scalar_mul(s_cdw[:, :], ones32[:, 0:128],
                                                       s_cd.ap()[:, 0:1]), sDVE)
            wait("scalar", sPE, sPE.n)
            hilo(s_cdw[:, :], s_cdwh[:, :], s_cdwl[:, :])
            op("vector", lambda e: e.tensor_tensor(s_pp[:, :], s_pp[:, :], s_cd[:, :], ALU.mult), sDVE)

            wait("tensor", sACT, sACT.n)
            for lt in range(2):
                base = 3 + c * CH + lt * 128
                for fq in range(4):
                    wait("tensor", sDVE, sDVE.n)
                    for k in range(4):
                        fc = fq * 4 + k
                        op("tensor", lambda e, fc=fc, k=k, base=base: e.transpose(
                            pb[7][:, k * 128:(k + 1) * 128], xbc_t(fc)[:, base:base + 128],
                            c_id.ap()), sPE if k == 3 else None)
                    wait("vector", sPE, sPE.n)
                    op("vector", lambda e, lt=lt, fq=fq: e.tensor_copy(
                        x4t[lt][:, fq * 512:(fq + 1) * 512], pb[7][:, :]), sDVE)
                wait("tensor", sDVE, sDVE.n)
                op("tensor", lambda e, base=base: e.transpose(
                    pb[7][:, 0:128], xbc_t(16)[:, base:base + 128], c_id.ap()), sPE)
                wait("vector", sPE, sPE.n)
                op("vector", lambda e, lt=lt: e.tensor_copy(
                    s_bmt[:, lt * 128:(lt + 1) * 128], pb[7][:, 0:128]), sDVE)
            t_x4 = sDVE.n
            wait("sync", sDVE, t_x4)
            dma("sync", scr_x4t[ck, 0], x4t[0], sIO)
            dma("sync", scr_x4t[ck, 1], x4t[1], sIO)

            for lt in range(2):
                t_q = sDVE.n
                wait("scalar", sDVE, t_q)
                wait("scalar", sPE, sPE.n)
                th = op("scalar", lambda e, lt=lt: e.activation(
                    s_qh[:, :], s_q[:, lt * 128:(lt + 1) * 128], AF.Copy), sACT)
                wait("vector", sACT, th)
                op("vector", lambda e, lt=lt: e.tensor_tensor(
                    s_ql[:, :], s_q[:, lt * 128:(lt + 1) * 128], s_qh[:, :], ALU.subtract), sDVE)
                wait("tensor", sDVE, sDVE.n)
                for slab in range(4):
                    wait("tensor", sDVE, sDVE.n)
                    op("tensor", lambda e, slab=slab: e.matmul(
                        pb[6][:, :], s_qh[:, :], c_rep.ap()[:, slab * 512:(slab + 1) * 512],
                        start=True, stop=False))
                    t_qt = op("tensor", lambda e, slab=slab: e.matmul(
                        pb[6][:, :], s_ql[:, :], c_rep.ap()[:, slab * 512:(slab + 1) * 512],
                        start=False, stop=True), sPE)
                    wait("vector", sPE, t_qt)
                    t_xdd = op("vector", lambda e, lt=lt, slab=slab: e.tensor_tensor(
                        xddb, x4t[lt][:, slab * 512:(slab + 1) * 512], pb[6][:, :], ALU.mult), sDVE)
                    wait("tensor", sDVE, t_xdd)
                    op("tensor", lambda e, lt=lt, slab=slab: e.matmul(
                        pb[slab % 4][:, :] if False else pb[[0, 1, 2, 3][slab]][:, :],
                        s_bmt[:, lt * 128:(lt + 1) * 128], xddb,
                        start=(lt == 0), stop=(lt == 1)), sPE if lt == 1 else None)
            t_states = sPE.n
            wait("scalar", sPE, t_states)
            dwait("scalar", sIO)
            wait("scalar", sDVE, sDVE.n)
            for slab in range(4):
                op("scalar", lambda e, slab=slab: e.activation(
                    stg[:, slab * 512:(slab + 1) * 512], pb[slab][:, :], AF.Copy), sACT)
            t_se = sACT.n
            wait("sync", sACT, t_se)
            dma("sync", scr_sta[ck], stg, sIO)
            run_prep_jobs(3)
            if DEBUG and ck == 0:
                dbg(dbg_sta, stg)
            if DEBUG and ck == 7:
                dbg(dbg_sta7, stg)
            wait("tensor", sDVE, sDVE.n)
            for slab in range(4):
                wait("tensor", sDVE, sDVE.n)
                op("tensor", lambda e, slab=slab: e.matmul(
                    pb[4][:, :], s_cdwh[:, :], c_rep.ap()[:, slab * 512:(slab + 1) * 512],
                    start=True, stop=False))
                t_cb = op("tensor", lambda e, slab=slab: e.matmul(
                    pb[4][:, :], s_cdwl[:, :], c_rep.ap()[:, slab * 512:(slab + 1) * 512],
                    start=False, stop=True), sPE)
                wait("vector", sPE, t_cb)
                wait("vector", sACT, t_se)
                op("vector", lambda e, slab=slab: e.tensor_tensor(
                    xddb, a_carry.ap()[:, slab * 512:(slab + 1) * 512], pb[4][:, :], ALU.mult))
                op("vector", lambda e, slab=slab: e.tensor_tensor(
                    a_carry.ap()[:, slab * 512:(slab + 1) * 512], xddb,
                    stg[:, slab * 512:(slab + 1) * 512], ALU.add), sDVE)

    run_prep_jobs(len(prep_jobs))
    t_sp = sDVE.n
    wait("sync", sDVE, t_sp)
    if DEBUG:
        dbg(dbg_spref, a_carry.ap())
        dbg(dbg_dtp, s_dtpf[:, :])
    dma("sync", cc_in[:, 0:2048], a_carry.ap(), sIO)
    dma("sync", cc_in[0:NH, 2048:2049], s_pp[:, :], sIO, slow=True)
    dwait("gpsimd", sIO)
    sCC.n += 1
    em("gpsimd", lambda e, s=sCC.h: e.collective_compute(
        "AllGather", ALU.bypass, ins=[cc_in], outs=[cc_out],
        replica_groups=[[0, 1, 2, 3], [4, 5, 6, 7]]).then_inc(s, 1))
    wait("sync", sCC, sCC.n)
    for j in range(4):
        dma("sync", gath[j], cc_out[j, :, 0:2048], sIO)
        dma("sync", s_pj[:, j:j + 1], cc_out[j, 0:NH, 2048:2049], sIO, slow=True)
    dwait("vector", sIO)
    dwait("vector", sTR)
    op("vector", lambda e: e.memset(a_carry.ap(), 0.0))
    acc = a_y.ap()[:, 0:2048]
    op("vector", lambda e: e.tensor_copy(acc, gath[0]), sDVE)
    for j in range(4):
        for slab in range(4):
            op("vector", lambda e, j=j, slab=slab: e.scalar_tensor_tensor(
                a_carry.ap()[:, slab * 512:(slab + 1) * 512],
                acc[:, slab * 512:(slab + 1) * 512], c_sel.ap()[:, j:j + 1],
                a_carry.ap()[:, slab * 512:(slab + 1) * 512], ALU.mult, ALU.add), sDVE)
        if j < 3:
            wait("vector", sPE, sPE.n)
            op("vector", lambda e, j=j: e.tensor_scalar_mul(
                s_cdw[:, :], ones32[:, 0:128], s_pj.ap()[:, j + 1:j + 2]), sDVE)
            wait("scalar", sPE, sPE.n)
            hilo(s_cdw[:, :], s_cdwh[:, :], s_cdwl[:, :])
            wait("tensor", sDVE, sDVE.n)
            for slab in range(4):
                wait("tensor", sDVE, sDVE.n)
                op("tensor", lambda e, slab=slab: e.matmul(
                    pb[4][:, :], s_cdwh[:, :], c_rep.ap()[:, slab * 512:(slab + 1) * 512],
                    start=True, stop=False))
                t_cb = op("tensor", lambda e, slab=slab: e.matmul(
                    pb[4][:, :], s_cdwl[:, :], c_rep.ap()[:, slab * 512:(slab + 1) * 512],
                    start=False, stop=True), sPE)
                wait("vector", sPE, t_cb)
                op("vector", lambda e, slab=slab: e.tensor_tensor(
                    xddb, acc[:, slab * 512:(slab + 1) * 512], pb[4][:, :], ALU.mult))
                op("vector", lambda e, j=j, slab=slab: e.tensor_tensor(
                    acc[:, slab * 512:(slab + 1) * 512], xddb,
                    gath[j + 1][:, slab * 512:(slab + 1) * 512], ALU.add), sDVE)

    if DEBUG:
        wait("sync", sDVE, sDVE.n)
        dbg(dbg_carry, a_carry.ap())

    wait("sync", sIO, sIO.n)
    dwait("sync", sGP)
    for st in range(NSUP):
        wait("sync", sDVE, sDVE.n)
        wait("sync", sACT, sACT.n)
        wait("sync", sPE, sPE.n)
        wait("sync", sST, sST.n)
        for f in range(NFIN):
            dma("sync", xn_hi[:, f * ST:(f + 1) * ST], scr_xnh[st, f], sIO)
            dma("sync", xn_lo[:, f * ST:(f + 1) * ST], scr_xnl[st, f], sIO)
        for fc in range(FCA):
            t_xbc = dma("sync", xbc_t(fc)[:, 3:3 + ST], scr_xbc[st, fc], sIO)

        for c in range(2):
            ck = st * 2 + c
            csk = s_csk.ap()[:, :]
            base = 3 + c * CH
            wait("vector", sACT, sACT.n)
            op("vector", lambda e, ck=ck: e.tensor_scalar_mul(
                s_sp1[:, 0:CH], s_dtpf[:, ck * CH:(ck + 1) * CH], p_A.ap()[:, 0:1]))
            op("vector", lambda e: e.tensor_tensor_scan(
                s_csk[:, :], ones32[:, :], s_sp1[:, 0:CH],
                0.0, ALU.mult, ALU.add), sDVE)
            wait("sync", sDVE, sDVE.n)
            wait("sync", sPE, sPE.n)
            dma("sync", x4t[0], scr_x4t[ck, 0], sIO)
            dma("sync", x4t[1], scr_x4t[ck, 1], sIO)
            dma("sync", stg, scr_sta[ck], sIO)
            dwait("vector", sIO)
            dwait("tensor", sIO)

            dwait("tensor", sIO)
            wait("tensor", sACT, sACT.n)
            wait("tensor", sDVE, sDVE.n)
            for m in range(2):
                wait("tensor", sACT, sACT.n)
                op("tensor", lambda e, csk=csk, m=m: e.transpose(
                    pb[7][:, 0:NH], csk[:, m * 128:(m + 1) * 128], c_id.ap()[0:NH, 0:NH]), sPE)
                t_tp = sPE.n
                wait("scalar", sPE, t_tp)
                op("scalar", lambda e, m=m: e.activation(
                    s_ctok.ap()[:, m * NH:(m + 1) * NH], pb[7][:, 0:NH], AF.Copy,
                    scale=-1.0), sACT)
            wait("tensor", sACT, sACT.n)
            wait("scalar", sGP, sGP.n)
            wait("scalar", sDVE, sDVE.n)
            th = op("scalar", lambda e, csk=csk: e.activation(s_cshi[:, :], csk, AF.Copy), sACT)
            wait("vector", sACT, th)
            op("vector", lambda e, csk=csk: e.tensor_tensor(s_csk1[:, :], csk, s_cshi[:, :], ALU.subtract))
            t1 = op("vector", lambda e: e.tensor_copy(s_cslo[:, :], s_csk1[:, :]), sDVE)
            op("vector", lambda e: e.tensor_tensor(s_csll[:, :], s_csk1[:, :], s_cslo[:, :], ALU.subtract), sDVE)
            t_spl = sDVE.n
            wait("gpsimd", sDVE, t_spl)
            wait("gpsimd", sPE, sPE.n)
            dma("gpsimd", csflat3[0:1, :], s_cshi[:, :], sGP)
            dma("gpsimd", csflat3[1:2, :], s_cslo[:, :], sGP)
            t_fl = dma("gpsimd", csflat3[2:3, :], s_csll[:, :], sGP)

            wait("tensor", sDVE, sDVE.n)
            for m in range(2):
                op("tensor", lambda e, m=m, base=base: e.matmul(
                    pb[5][:, m * CH:(m + 1) * CH], xbc_t(16)[:, base + m * 128:base + m * 128 + 128],
                    xbc_t(17)[:, base:base + CH], start=True, stop=True), sPE)
            t_sc = sPE.n

            wait("sync", sIO, sIO.n)
            for lt in range(2):
                tq = sDVE.n
                wait("scalar", sDVE, tq)
                wait("scalar", sPE, sPE.n)
                th = op("scalar", lambda e, ck=ck, lt=lt: e.activation(
                    s_hdtp[:, :], s_dtpf[:, ck * CH + lt * 128:ck * CH + (lt + 1) * 128],
                    AF.Copy), sACT)
                wait("vector", sACT, th)
                op("vector", lambda e, ck=ck, lt=lt: e.tensor_tensor(
                    s_ldtp[:, :], s_dtpf[:, ck * CH + lt * 128:ck * CH + (lt + 1) * 128],
                    s_hdtp[:, :], ALU.subtract), sDVE)
                wait("tensor", sDVE, sDVE.n)
                for slab in range(4):
                    wait("tensor", sDVE, sDVE.n)
                    op("tensor", lambda e, slab=slab: e.matmul(
                        pb[4][:, :], s_hdtp[:, :], c_rep.ap()[:, slab * 512:(slab + 1) * 512],
                        start=True, stop=False))
                    t_dt = op("tensor", lambda e, slab=slab: e.matmul(
                        pb[4][:, :], s_ldtp[:, :], c_rep.ap()[:, slab * 512:(slab + 1) * 512],
                        start=False, stop=True), sPE)
                    wait("vector", sPE, t_dt)
                    op("vector", lambda e, lt=lt, slab=slab: e.tensor_tensor(
                        x4t[lt][:, slab * 512:(slab + 1) * 512],
                        x4t[lt][:, slab * 512:(slab + 1) * 512], pb[4][:, :], ALU.mult), sDVE)
            t_xd = sDVE.n
            if DEBUG and ck == 0:
                wait("sync", sDVE, t_xd)
                dbg(dbg_x4t, x4t[0])

            dwait("tensor", sGP)
            emv = [(s_em0.ap(), s_em1.ap(), s_enb.ap(), s_mnb.ap()),
                   (a_big.ap()[:, 6144:6656], a_big.ap()[:, 6656:7168],
                    a_big.ap()[:, 7168:7680], a_big.ap()[:, 7680:8192])]
            dbk = [(pb[0], pb[1]), (pb[6], pb[7])]
            enb_bk = [pb[2], pb[4]]
            e_trk = [sACT.n, sACT.n]
            d_trk = [sDVE.n, sDVE.n]
            last_ystt = [0]
            y_info = {}
            for g in range(17):
                if g < 16:
                    p = g % 2
                    h0 = 2 * g
                    em0p, em1p, enbp, mnbp = emv[p]
                    wait("tensor", sACT, e_trk[p])
                    if g == 1:
                        wait("tensor", sDVE, t_xd)
                    for m in range(2):
                        op("tensor", lambda e, g=g, m=m, p=p: e.matmul(
                            dbk[p][m][:, :], c_ones3.ap(), csflat3[:, g * 512:(g + 1) * 512],
                            start=True, stop=False))
                        for j in range(2):
                            op("tensor", lambda e, m=m, j=j, p=p: e.matmul(
                                dbk[p][m][:, j * CH:(j + 1) * CH], c_idbf.ap(),
                                c_trimlt.ap()[:, m * CH:(m + 1) * CH],
                                start=False, stop=(j == 1)), sPE if j == 1 else None)
                    op("tensor", lambda e, g=g, p=p: e.matmul(
                        enb_bk[p][:, :], c_ones3.ap(), csflat3[:, g * 512:(g + 1) * 512],
                        start=True, stop=True), sPE)
                    t_d = sPE.n
                    wait("scalar", sPE, t_d)
                    wait("scalar", sDVE, d_trk[p])
                    for m, emb in ((0, em0p), (1, em1p)):
                        for j in range(2):
                            op("scalar", lambda e, m=m, j=j, emb=emb, h0=h0, p=p: e.activation(
                                emb[:, j * CH:(j + 1) * CH], dbk[p][m][:, j * CH:(j + 1) * CH],
                                AF.Exp, bias=s_ctok.ap()[:, m * NH + h0 + j:m * NH + h0 + j + 1]),
                                sACT)
                    t_e = op("scalar", lambda e, p=p, enbp=enbp: e.activation(enbp, enb_bk[p][:, :], AF.Exp), sACT)
                    e_trk[p] = t_e
                    wait("vector", sACT, t_e)
                    for m, emb in ((0, em0p), (1, em1p)):
                        op("vector", lambda e, m=m, emb=emb: e.tensor_tensor(
                            emb.rearrange("p (r c) -> p r c", r=2),
                            emb.rearrange("p (r c) -> p r c", r=2),
                            pb[5][:, m * CH:(m + 1) * CH].rearrange("p (r c) -> p r c", r=1)
                            .to_broadcast([128, 2, CH]), ALU.mult), sDVE)
                    t_mn = op("vector", lambda e, base=base, enbp=enbp, mnbp=mnbp: e.tensor_tensor(
                        mnbp.rearrange("p (r c) -> p r c", r=2),
                        enbp.rearrange("p (r c) -> p r c", r=2),
                        xbc_t(17)[:, base:base + CH].rearrange("p (r c) -> p r c", r=1)
                        .to_broadcast([128, 2, CH]), ALU.mult), sDVE)
                    d_trk[p] = t_mn
                    y_info[g] = (t_mn, em0p, em1p, mnbp, h0)
                    if DEBUG and ck == 0 and g == 0:
                        wait("sync", sDVE, sDVE.n)
                        dbg(dbg_em, s_em0.ap())
                        dbg(dbg_mn, s_mnb.ap())
                        wait("tensor", sST, sST.n)
                    if DEBUG and ck == 1 and g == 0:
                        wait("sync", sDVE, sDVE.n)
                        dbg(dbg_em1, s_em0.ap())
                        dbg(dbg_ctk, s_ctok.ap())
                        wait("tensor", sST, sST.n)
                if g >= 1:
                    gp = g - 1
                    t_mnp, em0p, em1p, mnbp, h0p = y_info.pop(gp)
                    wait("tensor", sDVE, max(t_mnp, last_ystt[0]))
                    for j in range(2):
                        h = h0p + j
                        op("tensor", lambda e, j=j, h=h, em0p=em0p: e.matmul(
                            pb[3][j * 64:(j + 1) * 64, 0:CH], x4t[0][:, h * 64:(h + 1) * 64],
                            em0p[:, j * CH:(j + 1) * CH], start=True, stop=False,
                            tile_position=(0, j * 64)))
                        op("tensor", lambda e, j=j, h=h, em1p=em1p: e.matmul(
                            pb[3][j * 64:(j + 1) * 64, 0:CH], x4t[1][:, h * 64:(h + 1) * 64],
                            em1p[:, j * CH:(j + 1) * CH], start=False, stop=False,
                            tile_position=(0, j * 64)))
                        op("tensor", lambda e, j=j, h=h, mnbp=mnbp: e.matmul(
                            pb[3][j * 64:(j + 1) * 64, 0:CH], a_carry.ap()[:, h * 64:(h + 1) * 64],
                            mnbp[:, j * CH:(j + 1) * CH], start=False, stop=True,
                            tile_position=(0, j * 64)), sPE if j == 1 else None)
                    t_y = sPE.n
                    wait("vector", sPE, t_y)
                    last_ystt[0] = op("vector", lambda e, gp=gp, c=c, base=base: e.scalar_tensor_tensor(
                        ybuf(gp)[:, c * CH:(c + 1) * CH], xbc_t(gp)[:, base:base + CH],
                        c_dexp.ap()[:, gp:gp + 1], pb[3][:, 0:CH], ALU.mult, ALU.add), sDVE)

            wait("scalar", sDVE, sDVE.n)
            op("scalar", lambda e, csk=csk: e.activation(s_cd[:, :], csk[:, CH - 1:CH], AF.Exp), sACT)
            wait("vector", sACT, sACT.n)
            op("vector", lambda e: e.tensor_scalar_mul(s_cdw[:, :], ones32[:, 0:128],
                                                       s_cd.ap()[:, 0:1]), sDVE)
            hilo(s_cdw[:, :], s_cdwh[:, :], s_cdwl[:, :])
            wait("tensor", sDVE, sDVE.n)
            for slab in range(4):
                wait("tensor", sDVE, sDVE.n)
                op("tensor", lambda e, slab=slab: e.matmul(
                    pb[4][:, :], s_cdwh[:, :], c_rep.ap()[:, slab * 512:(slab + 1) * 512],
                    start=True, stop=False))
                t_cb = op("tensor", lambda e, slab=slab: e.matmul(
                    pb[4][:, :], s_cdwl[:, :], c_rep.ap()[:, slab * 512:(slab + 1) * 512],
                    start=False, stop=True), sPE)
                wait("vector", sPE, t_cb)
                op("vector", lambda e, slab=slab: e.tensor_tensor(
                    xddb, a_carry.ap()[:, slab * 512:(slab + 1) * 512], pb[4][:, :], ALU.mult))
                op("vector", lambda e, slab=slab: e.tensor_tensor(
                    a_carry.ap()[:, slab * 512:(slab + 1) * 512], xddb,
                    stg[:, slab * 512:(slab + 1) * 512], ALU.add), sDVE)

        if DEBUG and st == 0:
            wait("sync", sDVE, sDVE.n)
            dbg(dbg_y, a_y.ap()[:, 0:FCZ * ST])

        zgroups = [list(range(0, 8)), list(range(8, 16))]
        z_sl = [0, 0]
        for fos_idx in zgroups:
            fos = [(f * 128, 128) for f in fos_idx]
            t_pe = proj(fos, [pb[j] for j in range(len(fos))], scrZ_hi, scrZ_lo,
                        lambda fin: xn_hi[:, fin * ST:(fin + 1) * ST],
                        lambda fin: xn_lo[:, fin * ST:(fin + 1) * ST], NFIN)
            wait("scalar", sPE, t_pe)
            for j, f in enumerate(fos_idx):
                sl = j % 2
                wait("scalar", sDVE, z_sl[sl])
                op("scalar", lambda e, j=j, sl=sl: e.activation(
                    stg[:, sl * ST:(sl + 1) * ST], pb[j][:, :], AF.Silu), sACT)
                wait("vector", sACT, sACT.n)
                z_sl[sl] = op("vector", lambda e, f=f, sl=sl: e.tensor_tensor(
                    ybuf(f), ybuf(f), stg[:, sl * ST:(sl + 1) * ST], ALU.mult), sDVE)
            wait("tensor", sDVE, sDVE.n)

        if DEBUG and st == 0:
            wait("sync", sDVE, sDVE.n)
            dbg(dbg_t, a_y.ap()[:, 0:FCZ * ST])
        if DEBUG and st == 1:
            wait("sync", sDVE, sDVE.n)
            dbg(dbg_t1, a_y.ap()[:, 0:FCZ * ST])

        wait("scalar", sDVE, sDVE.n)
        for f in range(FCZ):
            op("scalar", lambda e, f=f: e.activation(stg[:, 0:ST], ybuf(f), AF.Square), sACT)
            t_sq = sACT.n
            wait("tensor", sACT, t_sq)
            op("tensor", lambda e, f=f: e.matmul(
                pb[6][0:1, 0:ST], onescol.ap(), stg[:, 0:ST],
                start=(f == 0), stop=(f == FCZ - 1)), sPE)
            if f < FCZ - 1:
                wait("scalar", sPE, sPE.n)
        t_ssq = sPE.n
        wait("vector", sPE, t_ssq)
        op("vector", lambda e: e.tensor_scalar_mul(s_rms0[:, :], pb[6][0:1, 0:ST], 1.0 / DSSM), sDVE)
        wait("scalar", sDVE, sDVE.n)
        t_r = op("scalar", lambda e: e.activation(s_rms1[:, :], s_rms0[:, :],
                                                  AF.Sqrt, bias=np.float32(EPS)), sACT)
        wait("vector", sACT, t_r)
        t_r2 = op("vector", lambda e: e.reciprocal(s_rms0[:, :], s_rms1[:, :]), sDVE)
        wait("tensor", sDVE, t_r2)
        op("tensor", lambda e: e.matmul(pb[6][:, 0:ST], ones32[0:1, 0:128],
                                        s_rms0[:, :], start=True, stop=True), sPE)
        t_rbc = sPE.n
        wait("vector", sPE, t_rbc)
        for f in range(FCZ):
            op("vector", lambda e, f=f: e.scalar_tensor_tensor(
                ybuf(f), ybuf(f), c_normw.ap()[:, f:f + 1], pb[6][:, :], ALU.mult, ALU.mult), sDVE)
            hilo(ybuf(f), yn_hi[:, f * ST:(f + 1) * ST], yn_lo[:, f * ST:(f + 1) * ST])
        wait("tensor", sDVE, sDVE.n)

        ogroups = [list(range(0, 8))]
        for fos_idx in ogroups:
            fos = [(f * 128, 128) for f in fos_idx]
            t_pe = proj(fos, [pb[j] for j in range(len(fos))], scrO_hi, scrO_lo,
                        lambda fin: yn_hi[:, fin * ST:(fin + 1) * ST],
                        lambda fin: yn_lo[:, fin * ST:(fin + 1) * ST], FCZ)
            wait("scalar", sPE, t_pe)
            wait("vector", sPE, t_pe)
            for j, f in enumerate(fos_idx):
                if j % 2 == 0:
                    op("scalar", lambda e, j=j, f=f: e.activation(sb8(f), pb[j][:, :], AF.Copy), sACT)
                else:
                    op("vector", lambda e, j=j, f=f: e.tensor_copy(sb8(f), pb[j][:, :]), sDVE)
            wait("tensor", sACT, sACT.n)
            wait("tensor", sDVE, sDVE.n)

        if DEBUG and st == 1:
            wait("sync", sACT, sACT.n)
            dbg(dbg_s1, a_y.ap()[:, 0:FCS * ST])
        if DEBUG and st == 0:
            wait("sync", sACT, sACT.n)
            wait("gpsimd", sDVE, sDVE.n)
            wait("gpsimd", sACT, sACT.n)
            for f in range(FCZ):
                sGP.n += 16
                em("gpsimd", lambda e, f=f, s=sGP.h: e.dma_start(
                    out=dbg_yn[:, f * ST:(f + 1) * ST],
                    in_=yn_hi[:, f * ST:(f + 1) * ST]).then_inc(s, 16))
            dbg(dbg_s, a_y.ap()[:, 0:FCS * ST])

        for half in range(2):
            for fog in range(4):
                fos_idx = [half * 12 + fog * 3 + j for j in range(3)]
                lo_off = fos_idx[0] * 128
                w = 384
                wait("tensor", sACT, sACT.n)
                wait("tensor", sDVE, sDVE.n)
                wait("sync", sPE, sPE.n)
                wait("sync", sTR, sTR.n)
                pe_after2 = {}
                for fin in range(NFIN):
                    b = fin % 2
                    if fin >= 2:
                        wait("sync", sPE, pe_after2[fin - 2])
                    s_w = sWLa if b == 0 else sWLb
                    dma("sync", wbh[b][:, 0:w], scrG[fin, :, lo_off:lo_off + w], s_w)
                    dma("sync", wbh[b][:, w:2 * w], scrU[fin, :, lo_off:lo_off + w], s_w)
                    wait("tensor", s_w, s_w.n)
                    mh = xn_hi[:, fin * ST:(fin + 1) * ST]
                    ml = xn_lo[:, fin * ST:(fin + 1) * ST]
                    for j in range(3):
                        for gu in range(2):
                            o = gu * w + j * 128
                            pj = gu * 3 + j
                            em("tensor", lambda e, o=o, pj=pj, b=b, s=(fin == 0), mh=mh: e.matmul(
                                pb[pj][:, :], wbh[b][:, o:o + 128], mh, start=s, stop=False))
                            op("tensor", lambda e, o=o, pj=pj, b=b, p=(fin == NFIN - 1), ml=ml: e.matmul(
                                pb[pj][:, :], wbh[b][:, o:o + 128], ml, start=False, stop=p), sPE)
                    pe_after2[fin] = sPE.n
                t_pe = sPE.n
                wait("scalar", sPE, t_pe)
                for j in range(3):
                    fo = fog * 3 + j
                    wait("scalar", sDVE, sDVE.n)
                    op("scalar", lambda e, j=j: e.activation(xddb, pb[j][:, :], AF.Silu,
                                                             scale=float(QSC)), sACT)
                    t_sg2 = sACT.n
                    wait("vector", sACT, t_sg2)
                    op("vector", lambda e, j=j: e.scalar_tensor_tensor(
                        ffn_h32, pb[3 + j][:, :], float(QSC), xddb,
                        ALU.mult, ALU.mult), sDVE)
                    t_h = sDVE.n
                    wait("scalar", sDVE, t_h)
                    th = op("scalar", lambda e, fo=fo: e.activation(
                        hh[:, fo * ST:(fo + 1) * ST], ffn_h32, AF.Copy), sACT)
                    wait("vector", sACT, th)
                    op("vector", lambda e, fo=fo: e.tensor_tensor(
                        hl[:, fo * ST:(fo + 1) * ST], ffn_h32,
                        hh[:, fo * ST:(fo + 1) * ST], ALU.subtract), sDVE)
                wait("tensor", sDVE, sDVE.n)
            wait("tensor", sACT, sACT.n)
            wait("tensor", sDVE, sDVE.n)
            fos = [(d * 128, 128) for d in range(FCS)]
            t_dn = proj(fos, [pb[j] for j in range(FCS)], scrD, None,
                        lambda fin: hh[:, fin * ST:(fin + 1) * ST],
                        lambda fin: hl[:, fin * ST:(fin + 1) * ST], 12,
                        two_pass=True, fin_base=half * 12)
            if half == 0:
                wait("scalar", sPE, t_dn)
                wait("vector", sPE, t_dn)
                for do in range(FCS):
                    if do % 2 == 0:
                        op("scalar", lambda e, do=do: e.activation(
                            gacc(do), pb[do][:, :], AF.Copy), sACT)
                    else:
                        op("vector", lambda e, do=do: e.tensor_copy(
                            gacc(do), pb[do][:, :]), sDVE)
                wait("tensor", sACT, sACT.n)
                wait("tensor", sDVE, sDVE.n)
            else:
                wait("vector", sPE, t_dn)
                for do in range(FCS):
                    op("vector", lambda e, do=do: e.tensor_tensor(
                        gacc(do), gacc(do), pb[do][:, :], ALU.add), sDVE)
                wait("tensor", sDVE, sDVE.n)
        if DEBUG and st == 1:
            wait("sync", sDVE, sDVE.n)
            wait("sync", sACT, sACT.n)
            dbg(dbg_g1, a_y.ap()[:, FCS * ST:2 * FCS * ST])

        for do in range(FCS):
            wait("scalar", sDVE, sDVE.n)
            t_g = op("scalar", lambda e, do=do: e.activation(
                xddb, gacc(do), AF.Silu, scale=float(QSC)), sACT)
            wait("vector", sACT, t_g)
            op("vector", lambda e, do=do: e.tensor_tensor(
                sb8(do), sb8(do), xddb, ALU.mult), sDVE)
            wait("scalar", sDVE, sDVE.n)

        t_sg_done = sDVE.n
        wait("sync", sPE, sPE.n)
        for tg in range(4):
            wait("sync", sDVE, sDVE.n)
            dma("sync", w_xre, x_in[128 + st * ST + tg * 128:128 + st * ST + (tg + 1) * 128, :], sIO)
            cx = settle(sIO)
            wait("tensor", sDVE, t_sg_done)
            for fq in range(2):
                wait("tensor", sDVE, sDVE.n)
                for k in range(4):
                    fc = fq * 4 + k
                    op("tensor", lambda e, fc=fc, k=k, tg=tg: e.transpose(
                        pb[7][:, k * 128:(k + 1) * 128], sb8(fc)[:, tg * 128:(tg + 1) * 128],
                        c_id.ap()), sPE if k == 3 else None)
                t_tp = sPE.n
                wait("vector", sPE, t_tp)
                em("vector", lambda e, s=sIO.ch, t=cx: e.wait_ge(s, t))
                dwait("vector", sST)
                op("vector", lambda e, fq=fq: e.tensor_tensor(
                    w_ostage[:, fq * 512:(fq + 1) * 512], pb[7][:, :],
                    w_xre[:, fq * 512:(fq + 1) * 512], ALU.add), sDVE)
            t_o = sDVE.n
            wait("sync", sDVE, t_o)
            dma("sync", out_d[st * ST + tg * 128:st * ST + (tg + 1) * 128, :], w_ostage, sST)

    wait("sync", sST, sST.n)

    with nc.Block() as block:
        @block.sync
        def _(sync):
            for f in plan["sync"]:
                f(sync)

        @block.gpsimd
        def _(gpsimd):
            for f in plan["gpsimd"]:
                f(gpsimd)

        @block.scalar
        def _(scalar):
            for f in plan["scalar"]:
                f(scalar)

        @block.vector
        def _(vector):
            for f in plan["vector"]:
                f(vector)

        @block.tensor
        def _(tensor):
            for f in plan["tensor"]:
                f(tensor)

    es.close()
    return nc


def _make_consts():
    ident = np.eye(128, dtype=np.float32)
    identbf = np.eye(128, dtype=ml_dtypes.bfloat16)
    trimlt = np.zeros((128, 2 * CH), dtype=np.float32)
    for m in range(2):
        s_glob = m * 128 + np.arange(128)[:, None]
        l = np.arange(CH)[None, :]
        trimlt[:, m * CH:(m + 1) * CH] = np.where(l < s_glob, -1e30, 0.0)
    trimlt = trimlt.astype(ml_dtypes.bfloat16)
    rep = np.zeros((NH, DSSM), dtype=np.float32)
    for h in range(NH):
        rep[h, h * PD:(h + 1) * PD] = 1.0
    rep = rep.astype(ml_dtypes.bfloat16)
    ones3 = np.ones((3, 128), dtype=ml_dtypes.bfloat16)
    return ident, identbf, trimlt, rep, ones3


def kernel(L0_attn_inputs, L0_attn_W_in, L0_attn_conv_w, L0_attn_conv_b,
           L0_attn_dt_bias, L0_attn_A_log, L0_attn_D, L0_attn_norm_w,
           L0_attn_W_out, L0_ffn_w_gate, L0_ffn_w_up, L0_ffn_w_down):
    x = np.ascontiguousarray(np.asarray(L0_attn_inputs, dtype=np.float32))
    w_in = np.ascontiguousarray(np.asarray(L0_attn_W_in, dtype=np.float32))
    w_out = np.ascontiguousarray(np.asarray(L0_attn_W_out, dtype=np.float32))
    conv_w = np.asarray(L0_attn_conv_w, dtype=np.float32)
    conv_b = np.asarray(L0_attn_conv_b, dtype=np.float32)
    dt_bias = np.asarray(L0_attn_dt_bias, dtype=np.float32).reshape(NH, 1)
    a_log = np.asarray(L0_attn_A_log, dtype=np.float32).reshape(NH, 1)
    d_par = np.asarray(L0_attn_D, dtype=np.float32)
    norm_w = np.asarray(L0_attn_norm_w, dtype=np.float32)
    wg = np.ascontiguousarray(np.asarray(L0_ffn_w_gate, dtype=np.int32))
    wu = np.ascontiguousarray(np.asarray(L0_ffn_w_up, dtype=np.int32))
    wd = np.ascontiguousarray(np.asarray(L0_ffn_w_down, dtype=np.int32))

    convdiag = np.zeros((FCA, 4, 128, 128), dtype=np.float32)
    for fc in range(FCA):
        for k in range(4):
            np.fill_diagonal(convdiag[fc, k], conv_w[fc * 128:(fc + 1) * 128, k])
    convdiag = np.ascontiguousarray(convdiag.transpose(0, 2, 1, 3).reshape(FCA, 128, 512))
    convb_fm = np.ascontiguousarray(conv_b.reshape(FCA, 128).T)
    dexp_fm = np.ascontiguousarray(np.repeat(d_par, PD).reshape(FCZ, 128).T)
    normw_fm = np.ascontiguousarray(norm_w.reshape(FCZ, 128).T)
    ident, identbf, trimlt, rep, ones3 = _make_consts()

    nc = build_program()
    in_maps = []
    for core in range(8):
        b = core // 4
        q = core % 4
        if q == 0:
            halo = np.zeros((128, DM), dtype=np.float32)
        else:
            halo = x[b, q * TCORE - 128:q * TCORE]
        xc = np.concatenate([halo, x[b, q * TCORE:(q + 1) * TCORE]], axis=0)
        sel = np.zeros((128, 4), dtype=np.float32)
        if q > 0:
            sel[:, q - 1] = 1.0
        in_maps.append(dict(
            x=np.ascontiguousarray(xc), w_in=w_in, w_out=w_out,
            wg=wg, wu=wu, wd=wd,
            convdiag=convdiag, convb_fm=convb_fm, dtbias=dt_bias, alog=a_log,
            dexp_fm=dexp_fm, normw_fm=normw_fm, ident=ident, identbf=identbf,
            trimlt=trimlt, rep=rep, ones3=ones3, sel=sel))
    res = run_bass_kernel_spmd(nc, in_maps, core_ids=list(range(8)))
    out = np.zeros((2, 8192, DM), dtype=np.float32)
    for core in range(8):
        b = core // 4
        q = core % 4
        out[b, q * TCORE:(q + 1) * TCORE] = np.asarray(res.results[core]["out"])
    return out


# version 22 (final)
# speedup vs baseline: 1.1795x; 1.1795x over previous
import contextlib
import numpy as np
import ml_dtypes
import concourse.bass as bass
import concourse.mybir as mybir
from concourse.bass_utils import run_bass_kernel_spmd

F32 = mybir.dt.float32
BF16 = mybir.dt.bfloat16
I32 = mybir.dt.int32
AF = mybir.ActivationFunctionType
ALU = mybir.AluOpType

DM = 1024
DSSM = 2048
NH = 32
PD = 64
CONVD = 2304
DIP = 4384
HID = 3072
CH = 256
EPS = 1e-5
QSC = 1.0 / 127.0

TCORE = 2048
ST = 512
NSUP = TCORE // ST
FCA = CONVD // 128
FCZ = DSSM // 128
FCS = DM // 128
FCH = HID // 128
NFIN = DM // 128
NCK = TCORE // CH
W3 = ST + 3


DEBUG = False


def build_program():
    nc = bass.Bass("TRN2", target_bir_lowering=False)

    x_in = nc.dram_tensor("x", [128 + TCORE, DM], F32, kind="ExternalInput").ap()
    w_in = nc.dram_tensor("w_in", [DIP, DM], F32, kind="ExternalInput").ap()
    w_out = nc.dram_tensor("w_out", [DM, DSSM], F32, kind="ExternalInput").ap()
    wg_i = nc.dram_tensor("wg", [HID, DM], I32, kind="ExternalInput").ap()
    wu_i = nc.dram_tensor("wu", [HID, DM], I32, kind="ExternalInput").ap()
    wd_i = nc.dram_tensor("wd", [DM, HID], I32, kind="ExternalInput").ap()
    convdiag = nc.dram_tensor("convdiag", [FCA, 128, 512], F32, kind="ExternalInput").ap()
    convb_fm = nc.dram_tensor("convb_fm", [128, FCA], F32, kind="ExternalInput").ap()
    dtbias = nc.dram_tensor("dtbias", [NH, 1], F32, kind="ExternalInput").ap()
    alog = nc.dram_tensor("alog", [NH, 1], F32, kind="ExternalInput").ap()
    dexp_fm = nc.dram_tensor("dexp_fm", [128, FCZ], F32, kind="ExternalInput").ap()
    normw_fm = nc.dram_tensor("normw_fm", [128, FCZ], F32, kind="ExternalInput").ap()
    ident_in = nc.dram_tensor("ident", [128, 128], F32, kind="ExternalInput").ap()
    identbf_in = nc.dram_tensor("identbf", [128, 128], BF16, kind="ExternalInput").ap()
    trimlt_in = nc.dram_tensor("trimlt", [128, 2 * CH], BF16, kind="ExternalInput").ap()
    rep_in = nc.dram_tensor("rep", [NH, DSSM], BF16, kind="ExternalInput").ap()
    ones3_in = nc.dram_tensor("ones3", [3, 128], BF16, kind="ExternalInput").ap()
    sel_in = nc.dram_tensor("sel", [128, 4], F32, kind="ExternalInput").ap()

    out_d = nc.dram_tensor("out", [TCORE, DM], F32, kind="ExternalOutput").ap()
    if DEBUG:
        dbg_xbc = nc.dram_tensor("dbg_xbc", [FCA, 128, ST], F32, kind="ExternalOutput").ap()
        dbg_xnh = nc.dram_tensor("dbg_xnh", [NFIN, 128, ST], F32, kind="ExternalOutput").ap()
        dbg_dtp = nc.dram_tensor("dbg_dtp", [NH, TCORE], F32, kind="ExternalOutput").ap()
        dbg_sta = nc.dram_tensor("dbg_sta", [128, DSSM], F32, kind="ExternalOutput").ap()
        dbg_spref = nc.dram_tensor("dbg_spref", [128, DSSM], F32, kind="ExternalOutput").ap()
        dbg_carry = nc.dram_tensor("dbg_carry", [128, DSSM], F32, kind="ExternalOutput").ap()
        dbg_y = nc.dram_tensor("dbg_y", [128, FCZ * ST], F32, kind="ExternalOutput").ap()
        dbg_t = nc.dram_tensor("dbg_t", [128, FCZ * ST], F32, kind="ExternalOutput").ap()
        dbg_s = nc.dram_tensor("dbg_s", [128, FCS * ST], F32, kind="ExternalOutput").ap()
        dbg_g = nc.dram_tensor("dbg_g", [128, FCS * ST], F32, kind="ExternalOutput").ap()
        dbg_x4t = nc.dram_tensor("dbg_x4t", [128, DSSM], F32, kind="ExternalOutput").ap()
        dbg_wpl = nc.dram_tensor("dbg_wpl", [128, CONVD + 32], F32, kind="ExternalOutput").ap()
        dbg_raw = nc.dram_tensor("dbg_raw", [FCA, 128, ST + 3], F32, kind="ExternalOutput").ap()
        dbg_em = nc.dram_tensor("dbg_em", [128, 512], F32, kind="ExternalOutput").ap()
        dbg_mn = nc.dram_tensor("dbg_mn", [128, 512], F32, kind="ExternalOutput").ap()
        dbg_sta7 = nc.dram_tensor("dbg_sta7", [128, DSSM], F32, kind="ExternalOutput").ap()
        dbg_em1 = nc.dram_tensor("dbg_em1", [128, 512], F32, kind="ExternalOutput").ap()
        dbg_ctk = nc.dram_tensor("dbg_ctk", [128, 2 * NH], F32, kind="ExternalOutput").ap()
        dbg_y3 = nc.dram_tensor("dbg_y3", [128, 512], F32, kind="ExternalOutput").ap()
        dbg_yn = nc.dram_tensor("dbg_yn", [128, FCZ * ST], F32, kind="ExternalOutput").ap()
        dbg_s1 = nc.dram_tensor("dbg_s1", [128, FCS * ST], F32, kind="ExternalOutput").ap()
        dbg_g1 = nc.dram_tensor("dbg_g1", [128, FCS * ST], F32, kind="ExternalOutput").ap()
        dbg_t1 = nc.dram_tensor("dbg_t1", [128, FCZ * ST], F32, kind="ExternalOutput").ap()
        dbg_ln = nc.dram_tensor("dbg_ln", [128, 8], F32, kind="ExternalOutput").ap()
        dbg_xnt = nc.dram_tensor("dbg_xnt", [128, DM], F32, kind="ExternalOutput").ap()

    scrA_hi = nc.dram_tensor("scrA_hi", [NFIN, 128, CONVD + 32], BF16, kind="Internal").ap()
    scrA_lo = nc.dram_tensor("scrA_lo", [NFIN, 128, CONVD + 32], BF16, kind="Internal").ap()
    scrZ_hi = nc.dram_tensor("scrZ_hi", [NFIN, 128, DSSM], BF16, kind="Internal").ap()
    scrZ_lo = nc.dram_tensor("scrZ_lo", [NFIN, 128, DSSM], BF16, kind="Internal").ap()
    scrO_hi = nc.dram_tensor("scrO_hi", [FCZ, 128, DM], BF16, kind="Internal").ap()
    scrO_lo = nc.dram_tensor("scrO_lo", [FCZ, 128, DM], BF16, kind="Internal").ap()
    scrG = nc.dram_tensor("scrG", [NFIN, 128, HID], BF16, kind="Internal").ap()
    scrU = nc.dram_tensor("scrU", [NFIN, 128, HID], BF16, kind="Internal").ap()
    scrD = nc.dram_tensor("scrD", [FCH, 128, DM], BF16, kind="Internal").ap()
    scr_xnh = nc.dram_tensor("scr_xnh", [NSUP, NFIN, 128, ST], BF16, kind="Internal").ap()
    scr_xnl = nc.dram_tensor("scr_xnl", [NSUP, NFIN, 128, ST], BF16, kind="Internal").ap()
    scr_xbc = nc.dram_tensor("scr_xbc", [NSUP, FCA, 128, ST], F32, kind="Internal").ap()
    scr_x4t = nc.dram_tensor("scr_x4t", [NCK, 2, 128, DSSM], F32, kind="Internal").ap()
    scr_sta = nc.dram_tensor("scr_sta", [NCK, 128, DSSM], F32, kind="Internal").ap()
    cc_in = nc.dram_tensor("cc_in", [128, 2056], F32, kind="Internal").ap()
    cc_out = nc.dram_tensor("cc_out", [4, 128, 2056], F32, kind="Internal",
                            addr_space="Local").ap()

    es = contextlib.ExitStack()

    def sbt(name, shape, dt):
        return es.enter_context(nc.sbuf_tensor(name, shape, dt))

    c_id = sbt("c_id", [128, 128], F32)
    c_idbf = sbt("c_idbf", [128, 128], BF16)
    c_trimlt = sbt("c_trimlt", [128, 2 * CH], BF16)
    c_rep = sbt("c_rep", [NH, DSSM], BF16)
    c_ones3 = sbt("c_ones3", [3, 128], BF16)
    c_convb = sbt("c_convb", [128, FCA], F32)
    c_dexp = sbt("c_dexp", [128, FCZ], F32)
    c_normw = sbt("c_normw", [128, FCZ], F32)
    c_sel = sbt("c_sel", [128, 4], F32)
    c_dtb = sbt("c_dtb", [NH, 1], F32)
    c_alog = sbt("c_alog", [NH, 1], F32)
    p_A = sbt("p_A", [NH, 1], F32)
    ones32 = sbt("ones32", [NH, CH], F32)
    onescol = sbt("onescol", [128, 1], F32)

    a_xn = sbt("a_xn", [128, 4096], F32)
    a_xbc = sbt("a_xbc", [128, FCA * W3], F32)
    a_y = sbt("a_y", [128, FCZ * ST], F32)
    a_big = sbt("a_big", [128, 8192], F32)
    a_carry = sbt("a_carry", [128, DSSM], F32)
    a_wrk = sbt("a_wrk", [128, 4672], F32)

    s_dtpf = sbt("s_dtpf", [NH, TCORE], F32)
    s_csk = sbt("s_csk", [NH, CH], F32)
    s_sp1 = sbt("s_sp1", [NH, ST], F32)
    s_sp2 = sbt("s_sp2", [NH, ST], F32)
    s_dtr = sbt("s_dtr", [NH, ST], F32)
    s_hdtp = sbt("s_hdtp", [NH, 128], BF16)
    s_ldtp = sbt("s_ldtp", [NH, 128], BF16)
    s_csk1 = sbt("s_csk1", [NH, CH], F32)
    s_csk2 = sbt("s_csk2", [NH, CH], F32)
    s_cshi = sbt("s_cshi", [NH, CH], BF16)
    s_cslo = sbt("s_cslo", [NH, CH], BF16)
    s_csll = sbt("s_csll", [NH, CH], BF16)
    csflat3 = sbt("csflat3", [3, NH * CH], BF16)
    s_decl = sbt("s_decl", [NH, CH], F32)
    s_q = sbt("s_q", [NH, CH], F32)
    s_qh = sbt("s_qh", [NH, 128], BF16)
    s_ql = sbt("s_ql", [NH, 128], BF16)
    s_cd = sbt("s_cd", [NH, 1], F32)
    s_cdw = sbt("s_cdw", [NH, 128], F32)
    s_cdwh = sbt("s_cdwh", [NH, 128], BF16)
    s_cdwl = sbt("s_cdwl", [NH, 128], BF16)
    s_pp = sbt("s_pp", [NH, 1], F32)
    s_pj = sbt("s_pj", [NH, 4], F32)
    s_bmt = sbt("s_bmt", [128, 2 * 128], F32)
    s_ln = sbt("s_ln", [128, 8], F32)
    s_rms0 = sbt("s_rms0", [1, ST], F32)
    s_rms1 = sbt("s_rms1", [1, ST], F32)
    s_ctok = sbt("s_ctok", [128, 2 * NH], F32)
    s_enb = sbt("s_enb", [128, 512], F32)
    s_em0 = sbt("s_em0", [128, 512], F32)
    s_em1 = sbt("s_em1", [128, 512], F32)
    s_mnb = sbt("s_mnb", [128, 512], F32)
    s_halo = sbt("s_halo", [128, 3 * FCA], F32)
    prep_t = sbt("prep_t", [128, 2048], BF16)

    pb = [es.enter_context(nc.psum_tensor(f"pb{i}", [128, 512], F32)) for i in range(8)]

    class Sem:
        def __init__(self, name, queue=None):
            self.h = es.enter_context(nc.semaphore(name))
            self.n = 0
            self.queue = queue
            if queue is not None:
                self.ch = es.enter_context(nc.semaphore(name + "c"))
                self.cn = 0

    sWLa = Sem("sWLa", "sync")
    sWLb = Sem("sWLb", "sync")
    sIO = Sem("sIO", "sync")
    sGP = Sem("sGP", "gpsimd")
    sPE = Sem("sPE")
    sACT = Sem("sACT")
    sDVE = Sem("sDVE")
    sST = Sem("sST", "sync")
    sCC = Sem("sCC")
    sTR = Sem("sTR", "sync")
    sIO2 = Sem("sIO2", "scalar")
    sTR2 = Sem("sTR2", "scalar")

    plan = {k: [] for k in ["sync", "gpsimd", "scalar", "vector", "tensor"]}

    def em(eng, f):
        plan[eng].append(f)

    def wait(eng, sem, tgt):
        if tgt and tgt > 0:
            em(eng, lambda e, s=sem.h, t=tgt: e.wait_ge(s, t))

    def dma(q, out_ap, in_ap, sem, slow=False):
        sem.n += 16
        em(q, lambda e, o=out_ap, i=in_ap, s=sem.h, sl=slow: e.dma_start(
            out=o, in_=i, allow_slow_non_contiguous=sl).then_inc(s, 16))
        return sem.n

    def dmat(q, out_ap, in_ap, sem):
        sem.n += 16
        em(q, lambda e, o=out_ap, i=in_ap, s=sem.h: e.dma_start_transpose(o, i).then_inc(s, 16))
        return sem.n

    def settle(sem):
        wait(sem.queue, sem, sem.n)
        sem.cn += 1
        em(sem.queue, lambda e, s=sem.ch: e.nop().then_inc(s, 1))
        return sem.cn

    def dwait(eng, sem):
        t = settle(sem)
        em(eng, lambda e, s=sem.ch, t=t: e.wait_ge(s, t))

    def op(eng, fn, sem=None):
        if sem is not None:
            sem.n += 1
            em(eng, lambda e, f=fn, s=sem.h: f(e).then_inc(s, 1))
            return sem.n
        em(eng, lambda e, f=fn: f(e))
        return None

    def dbg(name_ap, src_ap):
        if DEBUG:
            dma("sync", name_ap, src_ap, sST)

    xn_hi = a_xn.ap()[:, 0:2048].bitcast(BF16)
    xn_lo = a_xn.ap()[:, 2048:4096].bitcast(BF16)
    hxn_hi = a_big.ap()[:, 6656:7168].bitcast(BF16)
    hxn_lo = a_big.ap()[:, 7168:7680].bitcast(BF16)

    def xbc_t(fc):
        return a_xbc.ap()[:, fc * W3:(fc + 1) * W3]

    yn_hi = a_xbc.ap()[:, 0:4096].bitcast(BF16)
    yn_lo = a_xbc.ap()[:, 4096:8192].bitcast(BF16)

    def ybuf(fc):
        return a_y.ap()[:, fc * ST:(fc + 1) * ST]

    def sb8(fc):
        return a_y.ap()[:, fc * ST:(fc + 1) * ST]

    def gacc(do):
        return a_y.ap()[:, (FCS + do) * ST:(FCS + do + 1) * ST]

    x4t = [a_big.ap()[:, 0:2048], a_big.ap()[:, 2048:4096]]
    stg = a_big.ap()[:, 4096:6144]
    xddb = a_big.ap()[:, 6144:6656]
    ffn_h32 = a_big.ap()[:, 6656:7168]
    prep_ffn = a_big.ap()[:, 6656:8192].bitcast(BF16)
    prep_ffn2 = a_big.ap()[:, 4096:5632].bitcast(BF16)
    gath = [a_big.ap()[:, j * 2048:(j + 1) * 2048] for j in range(4)]
    hh = a_big.ap()[:, 0:3072].bitcast(BF16)
    hl = a_big.ap()[:, 3072:6144].bitcast(BF16)

    w_xtok = a_wrk.ap()[:, 0:1024]
    w_dump = a_wrk.ap()[:, 1024:2048]
    w_xntok = a_wrk.ap()[:, 2048:3072]
    tokhi = a_wrk.ap()[:, 3072:3584].bitcast(BF16)
    toklo = a_wrk.ap()[:, 3584:4096].bitcast(BF16)
    prep_nat = a_wrk.ap()[:, 0:1024]
    prep_hi = a_wrk.ap()[:, 1024:1536].bitcast(BF16)
    prep_lo = a_wrk.ap()[:, 1536:2048].bitcast(BF16)
    prep_nat2 = a_wrk.ap()[:, 2048:3072]
    prep_hi2 = a_wrk.ap()[:, 3072:3584].bitcast(BF16)
    prep_lo2 = a_wrk.ap()[:, 3584:4096].bitcast(BF16)
    w_ostage = a_wrk.ap()[:, 0:1024]
    w_xre = a_wrk.ap()[:, 1024:2048]
    cdg2 = [a_wrk.ap()[:, 0:512], a_wrk.ap()[:, 512:1024]]
    wbh = [a_wrk.ap()[:, 0:1168].bitcast(BF16), a_wrk.ap()[:, 1168:2336].bitcast(BF16)]
    wbl = [a_wrk.ap()[:, 2336:3504].bitcast(BF16), a_wrk.ap()[:, 3504:4672].bitcast(BF16)]
    fbg = a_wrk.ap()[:, 0:1536].bitcast(BF16)
    fbu = a_wrk.ap()[:, 1536:3072].bitcast(BF16)

    for dst, src in [(c_id.ap(), ident_in), (c_idbf.ap(), identbf_in),
                     (c_trimlt.ap(), trimlt_in), (c_rep.ap(), rep_in),
                     (c_ones3.ap(), ones3_in), (c_convb.ap(), convb_fm),
                     (c_dexp.ap(), dexp_fm), (c_normw.ap(), normw_fm),
                     (c_sel.ap(), sel_in), (c_dtb.ap(), dtbias)]:
        dma("sync", dst, src, sIO)
    t_const = dma("sync", c_alog.ap(), alog, sIO)

    wait("vector", sIO, t_const)
    op("vector", lambda e: e.memset(ones32[:], 1.0))
    op("vector", lambda e: e.memset(onescol[:], 1.0))
    op("vector", lambda e: e.memset(a_carry.ap(), 0.0))
    op("vector", lambda e: e.memset(s_pp[:], 1.0), sDVE)
    wait("scalar", sIO, t_const)
    op("scalar", lambda e: e.activation(p_A[:], c_alog[:], AF.Exp))
    t_A = op("scalar", lambda e: e.activation(p_A[:], p_A[:], AF.Copy, scale=-1.0), sACT)

    def _bufset(base_ap, off):
        return dict(nat=base_ap[:, off:off + 1024],
                    hi=base_ap[:, off + 1024:off + 1536].bitcast(BF16),
                    lo=base_ap[:, off + 1536:off + 2048].bitcast(BF16))

    prep_t2 = a_y.ap()[:, 4096:6144].bitcast(BF16)
    PIPE_A = dict(q="sync", sio=sIO,
                  sets=[dict(_bufset(a_wrk.ap(), 0), bank=0, pt=prep_t, pt0=0,
                             natbf=prep_ffn[:, 0:1536]),
                        dict(_bufset(a_y.ap(), 0), bank=1, pt=prep_t, pt0=8,
                             natbf=prep_ffn[:, 1536:3072])])
    PIPE_B = dict(q="scalar", sio=sIO2,
                  sets=[dict(_bufset(a_wrk.ap(), 2048), bank=2, pt=prep_t2, pt0=0,
                             natbf=prep_ffn2[:, 0:1536]),
                        dict(_bufset(a_y.ap(), 2048), bank=3, pt=prep_t2, pt0=8,
                             natbf=prep_ffn2[:, 1536:3072])])

    def prep_tiles(P, S, pairs):
        assert len(pairs) <= 8
        q, sio = P["q"], P["sio"]
        wait("tensor", sACT, S.get("act_t", 0))
        wait("tensor", sDVE, S.get("dve_t", 0))
        wait("tensor", sACT, S.get("ev_t", 0))
        pbB = pb[S["bank"]].ap().bitcast(BF16)
        rr0 = pairs[0][2]
        for t, (s_ap, _, rr) in enumerate(pairs):
            op("tensor", lambda e, t=t, s_ap=s_ap, rr=rr, pbB=pbB, rr0=rr0: e.transpose(
                pbB[:, t * rr0:t * rr0 + rr],
                s_ap, c_idbf.ap()[0:s_ap.shape[0], 0:s_ap.shape[0]]),
                sPE if t == len(pairs) - 1 else None)
        S["tp_t"] = sPE.n
        wait("scalar", sPE, S["tp_t"])
        wait("scalar", sio, S.get("st_t", 0))
        wcols = len(pairs) * rr0
        op("scalar", lambda e, wcols=wcols, pbB=pbB, S=S: e.activation(
            S["pt"][:, S["pt0"] * 128:S["pt0"] * 128 + wcols],
            pbB[:, 0:wcols], AF.Copy), sACT)
        S["ev_t"] = sACT.n
        wait(q, sACT, S["ev_t"])
        for t, (_, dst, rr) in enumerate(pairs):
            dma(q, dst, S["pt"][:, S["pt0"] * 128 + t * rr0:S["pt0"] * 128 + t * rr0 + rr], sio)
        S["st_t"] = sio.n

    def _issue_load(P, S, job):
        kind = job[0]
        if kind == "f32":
            _, src_rows, rr, width, dh, dl, fo, po = job
            wait(P["q"], sPE, S.get("tp_t", 0))
            dma(P["q"], S["nat"][0:rr, 0:width], src_rows, P["sio"])
        else:
            _, src_rows, width, dst, fo, po = job
            wait("gpsimd", sPE, S.get("tp_t", 0))
            dma("gpsimd", S["natbf"][:, 0:width], src_rows, sGP)

    def _process(P, S, job):
        kind = job[0]
        if kind == "f32":
            _, src_rows, rr, width, dh, dl, fo, po = job
            dwait("scalar", P["sio"])
            wait("scalar", sPE, S.get("tp_t", 0))
            t_h = op("scalar", lambda e, r=rr, w=width, S=S: e.activation(
                S["hi"][0:r, 0:w], S["nat"][0:r, 0:w], AF.Copy), sACT)
            wait("vector", sACT, t_h)
            wait("vector", sPE, S.get("tp_t", 0))
            t_lo = op("vector", lambda e, r=rr, w=width, S=S: e.tensor_tensor(
                S["lo"][0:r, 0:w], S["nat"][0:r, 0:w], S["hi"][0:r, 0:w], ALU.subtract), sDVE)
            S["act_t"] = t_h
            S["dve_t"] = t_lo
            prep_tiles(P, S, [(S["hi"][0:rr, cc * 128:(cc + 1) * 128],
                               dh[po + cc, :, fo:fo + rr], rr)
                              for cc in range(width // 128)])
            prep_tiles(P, S, [(S["lo"][0:rr, cc * 128:(cc + 1) * 128],
                               dl[po + cc, :, fo:fo + rr], rr)
                              for cc in range(width // 128)])
        else:
            _, src_rows, width, dst, fo, po = job
            dwait("tensor", sGP)
            S["act_t"] = 0
            S["dve_t"] = 0
            for g0 in range(0, width // 128, 8):
                prep_tiles(P, S, [(S["natbf"][:, cc * 128:(cc + 1) * 128],
                                   dst[po + cc, :, fo:fo + 128], 128)
                                  for cc in range(g0, min(g0 + 8, width // 128))])

    def run_pipe(P, jobs):
        if jobs:
            _issue_load(P, P["sets"][0], jobs[0])
        for k, job in enumerate(jobs):
            if k + 1 < len(jobs):
                _issue_load(P, P["sets"][(k + 1) % 2], jobs[k + 1])
            _process(P, P["sets"][k % 2], job)

    jobsA, jobsB = [], []
    for blk in range(FCA):
        (jobsA if blk % 2 == 0 else jobsB).append(
            ("f32", w_in[2048 + blk * 128:2048 + (blk + 1) * 128, :], 128, 1024,
             scrA_hi, scrA_lo, blk * 128, 0))
    jobsA.append(("f32", w_in[4352:4384, :], 32, 1024, scrA_hi, scrA_lo, CONVD, 0))
    run_pipe(PIPE_A, jobsA)
    run_pipe(PIPE_B, jobsB)
    t_prepA = sIO.n

    jobsA, jobsB = [], []
    for blk in range(FCZ):
        (jobsA if blk % 2 == 0 else jobsB).append(
            ("f32", w_in[blk * 128:(blk + 1) * 128, :], 128, 1024, scrZ_hi, scrZ_lo, blk * 128, 0))
    for blk in range(FCS):
        jobsA.append(("f32", w_out[blk * 128:(blk + 1) * 128, 0:1024], 128, 1024,
                      scrO_hi, scrO_lo, blk * 128, 0))
        jobsB.append(("f32", w_out[blk * 128:(blk + 1) * 128, 1024:2048], 128, 1024,
                      scrO_hi, scrO_lo, blk * 128, 8))
    for blk in range(FCH):
        jobsA.append(("int", wg_i[blk * 128:(blk + 1) * 128, :], 1024, scrG, blk * 128, 0))
        jobsB.append(("int", wu_i[blk * 128:(blk + 1) * 128, :], 1024, scrU, blk * 128, 0))
    for blk in range(FCS):
        jobsA.append(("int", wd_i[blk * 128:(blk + 1) * 128, 0:1536], 1536, scrD, blk * 128, 0))
        jobsB.append(("int", wd_i[blk * 128:(blk + 1) * 128, 1536:3072], 1536, scrD, blk * 128, 12))
    run_pipe(PIPE_A, jobsA)
    run_pipe(PIPE_B, jobsB)
    t_prep_all = sIO.n
    t_prep_gp = sGP.n

    def ln_group(x_rows_ap, dst_hi_fm, dst_lo_fm, col_off, fm_w, dump=False):
        wait("sync", sPE, sPE.n)
        wait("sync", sACT, sACT.n)
        wait("sync", sDVE, sDVE.n)
        dma("sync", w_xtok, x_rows_ap, sIO)
        dwait("scalar", sIO)
        wait("scalar", sDVE, sDVE.n)
        op("scalar", lambda e: e.activation(w_dump, w_xtok, AF.Copy, accum_out=s_ln.ap()[:, 0:1]))
        t_s = op("scalar", lambda e: e.activation(w_dump, w_xtok, AF.Square, accum_out=s_ln.ap()[:, 1:2]), sACT)
        wait("vector", sACT, t_s)
        op("vector", lambda e: e.tensor_scalar_mul(s_ln.ap()[:, 2:3], s_ln.ap()[:, 0:1], 1.0 / DM))
        op("vector", lambda e: e.tensor_scalar_mul(s_ln.ap()[:, 3:4], s_ln.ap()[:, 1:2], 1.0 / DM))
        t_v = op("vector", lambda e: e.scalar_tensor_tensor(
            s_ln.ap()[:, 4:5], s_ln.ap()[:, 2:3], s_ln.ap()[:, 2:3], s_ln.ap()[:, 3:4],
            ALU.mult, ALU.subtract), sDVE)
        wait("scalar", sDVE, t_v)
        t_sd = op("scalar", lambda e: e.activation(s_ln.ap()[:, 5:6], s_ln.ap()[:, 4:5],
                                                   AF.Sqrt, bias=np.float32(EPS), scale=-1.0), sACT)
        wait("vector", sACT, t_sd)
        op("vector", lambda e: e.reciprocal(s_ln.ap()[:, 6:7], s_ln.ap()[:, 5:6]))
        t_xn = op("vector", lambda e: e.tensor_scalar(
            w_xntok, w_xtok, s_ln.ap()[:, 2:3], s_ln.ap()[:, 6:7],
            ALU.subtract, ALU.mult), sDVE)
        wait("scalar", sDVE, t_xn)
        dwait("scalar", sTR)
        t_h = op("scalar", lambda e: e.activation(tokhi, w_xntok, AF.Copy), sACT)
        wait("vector", sACT, t_h)
        dwait("vector", sTR)
        t_l = op("vector", lambda e: e.tensor_tensor(toklo, w_xntok, tokhi, ALU.subtract), sDVE)
        wait("sync", sACT, t_h)
        wait("sync", sDVE, t_l)
        if DEBUG and dump:
            dma("sync", dbg_ln, s_ln.ap(), sST)
            dma("sync", dbg_xnt, w_xntok, sST)
        for fc in range(NFIN):
            dmat("sync", dst_hi_fm[:, fc * fm_w + col_off:fc * fm_w + col_off + 128],
                 tokhi[:, fc * 128:(fc + 1) * 128], sTR)
            dmat("sync", dst_lo_fm[:, fc * fm_w + col_off:fc * fm_w + col_off + 128],
                 toklo[:, fc * 128:(fc + 1) * 128], sTR)
        return sTR.n

    def proj(fo_slices, psum_list, planes_hi, planes_lo, mov_hi_fn, mov_lo_fn,
             n_fin, two_pass=False, fin_base=0):
        lo_off = min(s[0] for s in fo_slices)
        hi_off = max(s[0] + s[1] for s in fo_slices)
        w = hi_off - lo_off
        wait("tensor", sACT, sACT.n)
        wait("tensor", sDVE, sDVE.n)
        wait("sync", sPE, sPE.n)
        wait("sync", sTR, sTR.n)
        pe_after = {}
        for fin in range(n_fin):
            b = fin % 2
            if fin >= 2:
                wait("sync", sPE, pe_after[fin - 2])
            s_w = sWLa if b == 0 else sWLb
            dma("sync", wbh[b][:, 0:w], planes_hi[fin_base + fin, :, lo_off:hi_off], s_w)
            if not two_pass:
                dma("sync", wbl[b][:, 0:w], planes_lo[fin_base + fin, :, lo_off:hi_off], s_w)
            wait("tensor", s_w, s_w.n)
            mh = mov_hi_fn(fin)
            ml = mov_lo_fn(fin)
            mw = mh.shape[-1]
            for j, (fo, fw) in enumerate(fo_slices):
                o = fo - lo_off
                stt = (fin == 0)
                stp = (fin == n_fin - 1)
                em("tensor", lambda e, j=j, b=b, o=o, fw=fw, s=stt, mh=mh, mw=mw: e.matmul(
                    psum_list[j][0:fw, 0:mw], wbh[b][:, o:o + fw], mh, start=s, stop=False))
                if two_pass:
                    op("tensor", lambda e, j=j, b=b, o=o, fw=fw, ml=ml, mw=mw, p=stp: e.matmul(
                        psum_list[j][0:fw, 0:mw], wbh[b][:, o:o + fw], ml, start=False, stop=p), sPE)
                else:
                    em("tensor", lambda e, j=j, b=b, o=o, fw=fw, ml=ml, mw=mw: e.matmul(
                        psum_list[j][0:fw, 0:mw], wbh[b][:, o:o + fw], ml, start=False, stop=False))
                    op("tensor", lambda e, j=j, b=b, o=o, fw=fw, mh=mh, mw=mw, p=stp: e.matmul(
                        psum_list[j][0:fw, 0:mw], wbl[b][:, o:o + fw], mh, start=False, stop=p), sPE)
            pe_after[fin] = sPE.n
        return sPE.n

    def hilo(src, dst_h, dst_l):
        wait("scalar", sDVE, sDVE.n)
        th = op("scalar", lambda e: e.activation(dst_h, src, AF.Copy), sACT)
        wait("vector", sACT, th)
        tl = op("vector", lambda e: e.tensor_tensor(dst_l, src, dst_h, ALU.subtract), sDVE)
        return th, tl

    wait("sync", sIO, t_prep_all)
    dwait("sync", sIO2)
    dwait("sync", sGP)
    if DEBUG:
        dma("sync", wbh[0][:, 0:2336], scrA_lo[0], sWLa)
        dwait("gpsimd", sWLa)
        sGP.n += 16
        em("gpsimd", lambda e, s=sGP.h: e.dma_start(
            out=dbg_wpl, in_=wbh[0][:, 0:2336]).then_inc(s, 16))
        wait("sync", sGP, sGP.n)
    ln_group(x_in[0:128, :], hxn_hi, hxn_lo, 0, 128)
    dwait("tensor", sTR)
    wait("tensor", sACT, t_A)
    for grp in range(3):
        fos = [(f * 128, 128) for f in range(grp * 6, grp * 6 + 6)]
        t_pe = proj(fos, [pb[j] for j in range(6)], scrA_hi, scrA_lo,
                    lambda fin: hxn_hi[:, fin * 128:(fin + 1) * 128],
                    lambda fin: hxn_lo[:, fin * 128:(fin + 1) * 128], NFIN)
        wait("scalar", sPE, t_pe)
        for j in range(6):
            fc = grp * 6 + j
            op("scalar", lambda e, j=j, fc=fc: e.activation(
                xbc_t(fc)[:, 0:3], pb[j][0:128, 125:128], AF.Copy), sACT)
        wait("tensor", sACT, sACT.n)

    for st in range(NSUP):
        for tg in range(4):
            t_tr = ln_group(x_in[128 + st * ST + tg * 128: 128 + st * ST + (tg + 1) * 128, :],
                            xn_hi, xn_lo, tg * 128, ST, dump=(st == 0 and tg == 0))
        wait("sync", sTR, sTR.n)
        for f in range(NFIN):
            dma("sync", scr_xnh[st, f], xn_hi[:, f * ST:(f + 1) * ST], sIO)
            dma("sync", scr_xnl[st, f], xn_lo[:, f * ST:(f + 1) * ST], sIO)
        if DEBUG and st == 0:
            dwait("gpsimd", sTR)
            for f in range(NFIN):
                sGP.n += 16
                em("gpsimd", lambda e, f=f, s=sGP.h: e.dma_start(
                    out=dbg_xnh[f], in_=xn_hi[:, f * ST:(f + 1) * ST]).then_inc(s, 16))

        dwait("tensor", sTR)
        dwait("scalar", sIO)
        groups = [[(f * 128, 128) for f in range(0, 8)],
                  [(f * 128, 128) for f in range(8, 16)],
                  [(16 * 128, 128), (17 * 128, 128), (CONVD, 32)]]
        for gi, fos in enumerate(groups):
            t_pe = proj(fos, [pb[j] for j in range(len(fos))], scrA_hi, scrA_lo,
                        lambda fin: xn_hi[:, fin * ST:(fin + 1) * ST],
                        lambda fin: xn_lo[:, fin * ST:(fin + 1) * ST], NFIN)
            wait("scalar", sPE, t_pe)
            wait("vector", sPE, t_pe)
            nfc = 8 if gi < 2 else 2
            for j in range(nfc):
                fc = gi * 8 + j
                if j % 2 == 0:
                    op("scalar", lambda e, j=j, fc=fc: e.activation(
                        xbc_t(fc)[:, 3:3 + ST], pb[j][:, 0:ST], AF.Copy), sACT)
                else:
                    op("vector", lambda e, j=j, fc=fc: e.tensor_copy(
                        xbc_t(fc)[:, 3:3 + ST], pb[j][:, 0:ST]), sDVE)
            if gi == 2:
                wait("vector", sACT, sACT.n)
                op("vector", lambda e: e.tensor_copy(s_dtr[:, :], pb[2][0:NH, 0:ST]), sDVE)
            wait("tensor", sACT, sACT.n)
            wait("tensor", sDVE, sDVE.n)

        t_evac = sACT.n
        if DEBUG and st == 0:
            wait("sync", sACT, sACT.n)
            wait("sync", sDVE, sDVE.n)
            for fc in range(FCA):
                dbg(dbg_raw[fc], xbc_t(fc)[:, 0:ST + 3])
        wait("sync", sPE, sPE.n)
        conv_silu_t = [sACT.n, sACT.n]
        dma("sync", cdg2[0], convdiag[0], sWLa)
        for fc in range(FCA):
            b = fc % 2
            if st > 0:
                wait("vector", sACT, t_evac)
                op("vector", lambda e, fc=fc: e.tensor_copy(
                    xbc_t(fc)[:, 0:3], s_halo.ap()[:, fc * 3:fc * 3 + 3]), sDVE)
            wait("vector", sACT, t_evac)
            t_tail = op("vector", lambda e, fc=fc: e.tensor_copy(
                s_halo.ap()[:, fc * 3:fc * 3 + 3], xbc_t(fc)[:, ST:ST + 3]), sDVE)
            s_w = sWLa if b == 0 else sWLb
            t_wl = s_w.n
            if fc + 1 < FCA:
                wait("sync", sPE, sPE.n)
                dma("sync", cdg2[(fc + 1) % 2], convdiag[fc + 1],
                    sWLb if b == 0 else sWLa)
            wait("tensor", s_w, t_wl)
            wait("tensor", sDVE, t_tail)
            cb = 6 + (fc % 2)
            wait("tensor", sACT, conv_silu_t[fc % 2])
            for k in range(4):
                op("tensor", lambda e, fc=fc, k=k, b=b, cb=cb: e.matmul(
                    pb[cb][:, 0:ST], cdg2[b][:, k * 128:(k + 1) * 128],
                    xbc_t(fc)[:, k:k + ST], start=(k == 0), stop=(k == 3)),
                    sPE if k == 3 else None)
            t_mm = sPE.n
            wait("scalar", sPE, t_mm)
            op("scalar", lambda e, fc=fc, cb=cb: e.activation(
                xbc_t(fc)[:, 3:3 + ST], pb[cb][:, 0:ST], AF.Silu,
                bias=c_convb.ap()[:, fc:fc + 1]), sACT)
            conv_silu_t[fc % 2] = sACT.n
        t_conv = sACT.n
        wait("sync", sACT, t_conv)
        for fc in range(FCA):
            dma("sync", scr_xbc[st, fc], xbc_t(fc)[:, 3:3 + ST], sIO)
        if DEBUG and st == 0:
            for fc in range(FCA):
                dbg(dbg_xbc[fc], xbc_t(fc)[:, 3:3 + ST])

        wait("scalar", sDVE, sDVE.n)
        op("scalar", lambda e: e.activation(s_sp1[:, :], s_dtr[:, :], AF.Abs,
                                            bias=c_dtb.ap()[:, 0:1]))
        op("scalar", lambda e: e.activation(s_sp1[:, :], s_sp1[:, :], AF.Exp, scale=-1.0))
        op("scalar", lambda e: e.activation(s_sp1[:, :], s_sp1[:, :], AF.Ln, bias=np.float32(1.0)))
        t_sp = op("scalar", lambda e: e.activation(s_sp2[:, :], s_dtr[:, :], AF.Relu,
                                                   bias=c_dtb.ap()[:, 0:1]), sACT)
        wait("vector", sACT, t_sp)
        op("vector", lambda e, st=st: e.tensor_tensor(
            s_dtpf[:, st * ST:(st + 1) * ST], s_sp1[:, :], s_sp2[:, :], ALU.add), sDVE)
        op("vector", lambda e, st=st: e.tensor_scalar_mul(
            s_sp1[:, :], s_dtpf[:, st * ST:(st + 1) * ST], p_A.ap()[:, 0:1]), sDVE)

        for c in range(2):
            ck = st * 2 + c
            csk = s_csk.ap()[:, :]
            wait("vector", sACT, sACT.n)
            op("vector", lambda e, c=c: e.tensor_tensor_scan(
                s_csk[:, :], ones32[:, :], s_sp1[:, c * CH:(c + 1) * CH],
                0.0, ALU.mult, ALU.add), sDVE)
            t_cs = sDVE.n
            wait("scalar", sDVE, t_cs)
            op("scalar", lambda e, csk=csk: e.activation(s_cd[:, :], csk[:, CH - 1:CH], AF.Exp))
            t_decl = op("scalar", lambda e, csk=csk: e.activation(
                s_decl[:, :], csk, AF.Exp, bias=csk[:, CH - 1:CH], scale=-1.0), sACT)
            wait("vector", sACT, t_decl)
            op("vector", lambda e, ck=ck, c=c: e.tensor_tensor(
                s_q[:, :], s_dtpf[:, ck * CH:(ck + 1) * CH], s_decl[:, :], ALU.mult))
            wait("vector", sPE, sPE.n)
            op("vector", lambda e: e.tensor_scalar_mul(s_cdw[:, :], ones32[:, 0:128],
                                                       s_cd.ap()[:, 0:1]), sDVE)
            wait("scalar", sPE, sPE.n)
            hilo(s_cdw[:, :], s_cdwh[:, :], s_cdwl[:, :])
            op("vector", lambda e: e.tensor_tensor(s_pp[:, :], s_pp[:, :], s_cd[:, :], ALU.mult), sDVE)

            wait("tensor", sACT, sACT.n)
            for lt in range(2):
                base = 3 + c * CH + lt * 128
                for fq in range(4):
                    wait("tensor", sDVE, sDVE.n)
                    for k in range(4):
                        fc = fq * 4 + k
                        op("tensor", lambda e, fc=fc, k=k, base=base: e.transpose(
                            pb[7][:, k * 128:(k + 1) * 128], xbc_t(fc)[:, base:base + 128],
                            c_id.ap()), sPE if k == 3 else None)
                    wait("vector", sPE, sPE.n)
                    op("vector", lambda e, lt=lt, fq=fq: e.tensor_copy(
                        x4t[lt][:, fq * 512:(fq + 1) * 512], pb[7][:, :]), sDVE)
                wait("tensor", sDVE, sDVE.n)
                op("tensor", lambda e, base=base: e.transpose(
                    pb[7][:, 0:128], xbc_t(16)[:, base:base + 128], c_id.ap()), sPE)
                wait("vector", sPE, sPE.n)
                op("vector", lambda e, lt=lt: e.tensor_copy(
                    s_bmt[:, lt * 128:(lt + 1) * 128], pb[7][:, 0:128]), sDVE)
            t_x4 = sDVE.n
            wait("sync", sDVE, t_x4)
            dma("sync", scr_x4t[ck, 0], x4t[0], sIO)
            dma("sync", scr_x4t[ck, 1], x4t[1], sIO)

            for lt in range(2):
                t_q = sDVE.n
                wait("scalar", sDVE, t_q)
                wait("scalar", sPE, sPE.n)
                th = op("scalar", lambda e, lt=lt: e.activation(
                    s_qh[:, :], s_q[:, lt * 128:(lt + 1) * 128], AF.Copy), sACT)
                wait("vector", sACT, th)
                op("vector", lambda e, lt=lt: e.tensor_tensor(
                    s_ql[:, :], s_q[:, lt * 128:(lt + 1) * 128], s_qh[:, :], ALU.subtract), sDVE)
                wait("tensor", sDVE, sDVE.n)
                for slab in range(4):
                    wait("tensor", sDVE, sDVE.n)
                    op("tensor", lambda e, slab=slab: e.matmul(
                        pb[6][:, :], s_qh[:, :], c_rep.ap()[:, slab * 512:(slab + 1) * 512],
                        start=True, stop=False))
                    t_qt = op("tensor", lambda e, slab=slab: e.matmul(
                        pb[6][:, :], s_ql[:, :], c_rep.ap()[:, slab * 512:(slab + 1) * 512],
                        start=False, stop=True), sPE)
                    wait("vector", sPE, t_qt)
                    t_xdd = op("vector", lambda e, lt=lt, slab=slab: e.tensor_tensor(
                        xddb, x4t[lt][:, slab * 512:(slab + 1) * 512], pb[6][:, :], ALU.mult), sDVE)
                    wait("tensor", sDVE, t_xdd)
                    op("tensor", lambda e, lt=lt, slab=slab: e.matmul(
                        pb[slab % 4][:, :] if False else pb[[0, 1, 2, 3][slab]][:, :],
                        s_bmt[:, lt * 128:(lt + 1) * 128], xddb,
                        start=(lt == 0), stop=(lt == 1)), sPE if lt == 1 else None)
            t_states = sPE.n
            wait("scalar", sPE, t_states)
            dwait("scalar", sIO)
            wait("scalar", sDVE, sDVE.n)
            for slab in range(4):
                op("scalar", lambda e, slab=slab: e.activation(
                    stg[:, slab * 512:(slab + 1) * 512], pb[slab][:, :], AF.Copy), sACT)
            t_se = sACT.n
            wait("sync", sACT, t_se)
            dma("sync", scr_sta[ck], stg, sIO)
            if DEBUG and ck == 0:
                dbg(dbg_sta, stg)
            if DEBUG and ck == 7:
                dbg(dbg_sta7, stg)
            wait("tensor", sDVE, sDVE.n)
            for slab in range(4):
                wait("tensor", sDVE, sDVE.n)
                op("tensor", lambda e, slab=slab: e.matmul(
                    pb[4][:, :], s_cdwh[:, :], c_rep.ap()[:, slab * 512:(slab + 1) * 512],
                    start=True, stop=False))
                t_cb = op("tensor", lambda e, slab=slab: e.matmul(
                    pb[4][:, :], s_cdwl[:, :], c_rep.ap()[:, slab * 512:(slab + 1) * 512],
                    start=False, stop=True), sPE)
                wait("vector", sPE, t_cb)
                wait("vector", sACT, t_se)
                op("vector", lambda e, slab=slab: e.tensor_tensor(
                    xddb, a_carry.ap()[:, slab * 512:(slab + 1) * 512], pb[4][:, :], ALU.mult))
                op("vector", lambda e, slab=slab: e.tensor_tensor(
                    a_carry.ap()[:, slab * 512:(slab + 1) * 512], xddb,
                    stg[:, slab * 512:(slab + 1) * 512], ALU.add), sDVE)

    t_sp = sDVE.n
    wait("sync", sDVE, t_sp)
    if DEBUG:
        dbg(dbg_spref, a_carry.ap())
        dbg(dbg_dtp, s_dtpf[:, :])
    dma("sync", cc_in[:, 0:2048], a_carry.ap(), sIO)
    dma("sync", cc_in[0:NH, 2048:2049], s_pp[:, :], sIO, slow=True)
    dwait("gpsimd", sIO)
    sCC.n += 1
    em("gpsimd", lambda e, s=sCC.h: e.collective_compute(
        "AllGather", ALU.bypass, ins=[cc_in], outs=[cc_out],
        replica_groups=[[0, 1, 2, 3], [4, 5, 6, 7]]).then_inc(s, 1))
    wait("sync", sCC, sCC.n)
    for j in range(4):
        dma("sync", gath[j], cc_out[j, :, 0:2048], sIO)
        dma("sync", s_pj[:, j:j + 1], cc_out[j, 0:NH, 2048:2049], sIO, slow=True)
    dwait("vector", sIO)
    op("vector", lambda e: e.memset(a_carry.ap(), 0.0))
    acc = a_y.ap()[:, 0:2048]
    op("vector", lambda e: e.tensor_copy(acc, gath[0]), sDVE)
    for j in range(4):
        for slab in range(4):
            op("vector", lambda e, j=j, slab=slab: e.scalar_tensor_tensor(
                a_carry.ap()[:, slab * 512:(slab + 1) * 512],
                acc[:, slab * 512:(slab + 1) * 512], c_sel.ap()[:, j:j + 1],
                a_carry.ap()[:, slab * 512:(slab + 1) * 512], ALU.mult, ALU.add), sDVE)
        if j < 3:
            wait("vector", sPE, sPE.n)
            op("vector", lambda e, j=j: e.tensor_scalar_mul(
                s_cdw[:, :], ones32[:, 0:128], s_pj.ap()[:, j + 1:j + 2]), sDVE)
            wait("scalar", sPE, sPE.n)
            hilo(s_cdw[:, :], s_cdwh[:, :], s_cdwl[:, :])
            wait("tensor", sDVE, sDVE.n)
            for slab in range(4):
                wait("tensor", sDVE, sDVE.n)
                op("tensor", lambda e, slab=slab: e.matmul(
                    pb[4][:, :], s_cdwh[:, :], c_rep.ap()[:, slab * 512:(slab + 1) * 512],
                    start=True, stop=False))
                t_cb = op("tensor", lambda e, slab=slab: e.matmul(
                    pb[4][:, :], s_cdwl[:, :], c_rep.ap()[:, slab * 512:(slab + 1) * 512],
                    start=False, stop=True), sPE)
                wait("vector", sPE, t_cb)
                op("vector", lambda e, slab=slab: e.tensor_tensor(
                    xddb, acc[:, slab * 512:(slab + 1) * 512], pb[4][:, :], ALU.mult))
                op("vector", lambda e, j=j, slab=slab: e.tensor_tensor(
                    acc[:, slab * 512:(slab + 1) * 512], xddb,
                    gath[j + 1][:, slab * 512:(slab + 1) * 512], ALU.add), sDVE)

    if DEBUG:
        wait("sync", sDVE, sDVE.n)
        dbg(dbg_carry, a_carry.ap())

    for st in range(NSUP):
        wait("sync", sDVE, sDVE.n)
        wait("sync", sACT, sACT.n)
        wait("sync", sPE, sPE.n)
        wait("sync", sST, sST.n)
        for f in range(NFIN):
            dma("sync", xn_hi[:, f * ST:(f + 1) * ST], scr_xnh[st, f], sIO)
            dma("sync", xn_lo[:, f * ST:(f + 1) * ST], scr_xnl[st, f], sIO)
        for fc in range(FCA):
            t_xbc = dma("sync", xbc_t(fc)[:, 3:3 + ST], scr_xbc[st, fc], sIO)

        for c in range(2):
            ck = st * 2 + c
            csk = s_csk.ap()[:, :]
            base = 3 + c * CH
            wait("vector", sACT, sACT.n)
            op("vector", lambda e, ck=ck: e.tensor_scalar_mul(
                s_sp1[:, 0:CH], s_dtpf[:, ck * CH:(ck + 1) * CH], p_A.ap()[:, 0:1]))
            op("vector", lambda e: e.tensor_tensor_scan(
                s_csk[:, :], ones32[:, :], s_sp1[:, 0:CH],
                0.0, ALU.mult, ALU.add), sDVE)
            wait("sync", sDVE, sDVE.n)
            wait("sync", sPE, sPE.n)
            dma("sync", x4t[0], scr_x4t[ck, 0], sIO)
            dma("sync", x4t[1], scr_x4t[ck, 1], sIO)
            dma("sync", stg, scr_sta[ck], sIO)
            dwait("vector", sIO)
            dwait("tensor", sIO)

            dwait("tensor", sIO)
            wait("tensor", sACT, sACT.n)
            wait("tensor", sDVE, sDVE.n)
            for m in range(2):
                wait("tensor", sACT, sACT.n)
                op("tensor", lambda e, csk=csk, m=m: e.transpose(
                    pb[7][:, 0:NH], csk[:, m * 128:(m + 1) * 128], c_id.ap()[0:NH, 0:NH]), sPE)
                t_tp = sPE.n
                wait("scalar", sPE, t_tp)
                op("scalar", lambda e, m=m: e.activation(
                    s_ctok.ap()[:, m * NH:(m + 1) * NH], pb[7][:, 0:NH], AF.Copy,
                    scale=-1.0), sACT)
            wait("tensor", sACT, sACT.n)
            wait("scalar", sGP, sGP.n)
            wait("scalar", sDVE, sDVE.n)
            th = op("scalar", lambda e, csk=csk: e.activation(s_cshi[:, :], csk, AF.Copy), sACT)
            wait("vector", sACT, th)
            op("vector", lambda e, csk=csk: e.tensor_tensor(s_csk1[:, :], csk, s_cshi[:, :], ALU.subtract))
            t1 = op("vector", lambda e: e.tensor_copy(s_cslo[:, :], s_csk1[:, :]), sDVE)
            op("vector", lambda e: e.tensor_tensor(s_csll[:, :], s_csk1[:, :], s_cslo[:, :], ALU.subtract), sDVE)
            t_spl = sDVE.n
            wait("gpsimd", sDVE, t_spl)
            wait("gpsimd", sPE, sPE.n)
            dma("gpsimd", csflat3[0:1, :], s_cshi[:, :], sGP)
            dma("gpsimd", csflat3[1:2, :], s_cslo[:, :], sGP)
            t_fl = dma("gpsimd", csflat3[2:3, :], s_csll[:, :], sGP)

            wait("tensor", sDVE, sDVE.n)
            for m in range(2):
                op("tensor", lambda e, m=m, base=base: e.matmul(
                    pb[5][:, m * CH:(m + 1) * CH], xbc_t(16)[:, base + m * 128:base + m * 128 + 128],
                    xbc_t(17)[:, base:base + CH], start=True, stop=True), sPE)
            t_sc = sPE.n

            wait("sync", sIO, sIO.n)
            for lt in range(2):
                tq = sDVE.n
                wait("scalar", sDVE, tq)
                wait("scalar", sPE, sPE.n)
                th = op("scalar", lambda e, ck=ck, lt=lt: e.activation(
                    s_hdtp[:, :], s_dtpf[:, ck * CH + lt * 128:ck * CH + (lt + 1) * 128],
                    AF.Copy), sACT)
                wait("vector", sACT, th)
                op("vector", lambda e, ck=ck, lt=lt: e.tensor_tensor(
                    s_ldtp[:, :], s_dtpf[:, ck * CH + lt * 128:ck * CH + (lt + 1) * 128],
                    s_hdtp[:, :], ALU.subtract), sDVE)
                wait("tensor", sDVE, sDVE.n)
                for slab in range(4):
                    wait("tensor", sDVE, sDVE.n)
                    op("tensor", lambda e, slab=slab: e.matmul(
                        pb[4][:, :], s_hdtp[:, :], c_rep.ap()[:, slab * 512:(slab + 1) * 512],
                        start=True, stop=False))
                    t_dt = op("tensor", lambda e, slab=slab: e.matmul(
                        pb[4][:, :], s_ldtp[:, :], c_rep.ap()[:, slab * 512:(slab + 1) * 512],
                        start=False, stop=True), sPE)
                    wait("vector", sPE, t_dt)
                    op("vector", lambda e, lt=lt, slab=slab: e.tensor_tensor(
                        x4t[lt][:, slab * 512:(slab + 1) * 512],
                        x4t[lt][:, slab * 512:(slab + 1) * 512], pb[4][:, :], ALU.mult), sDVE)
            t_xd = sDVE.n
            if DEBUG and ck == 0:
                wait("sync", sDVE, t_xd)
                dbg(dbg_x4t, x4t[0])

            dwait("tensor", sGP)
            emv = [(s_em0.ap(), s_em1.ap(), s_enb.ap(), s_mnb.ap()),
                   (a_big.ap()[:, 6144:6656], a_big.ap()[:, 6656:7168],
                    a_big.ap()[:, 7168:7680], a_big.ap()[:, 7680:8192])]
            dbk = [(pb[0], pb[1]), (pb[6], pb[7])]
            enb_bk = [pb[2], pb[4]]
            e_trk = [sACT.n, sACT.n]
            d_trk = [sDVE.n, sDVE.n]
            last_ystt = [0]
            y_info = {}
            for g in range(17):
                if g < 16:
                    p = g % 2
                    h0 = 2 * g
                    em0p, em1p, enbp, mnbp = emv[p]
                    wait("tensor", sACT, e_trk[p])
                    if g == 1:
                        wait("tensor", sDVE, t_xd)
                    for m in range(2):
                        op("tensor", lambda e, g=g, m=m, p=p: e.matmul(
                            dbk[p][m][:, :], c_ones3.ap(), csflat3[:, g * 512:(g + 1) * 512],
                            start=True, stop=False))
                        for j in range(2):
                            op("tensor", lambda e, m=m, j=j, p=p: e.matmul(
                                dbk[p][m][:, j * CH:(j + 1) * CH], c_idbf.ap(),
                                c_trimlt.ap()[:, m * CH:(m + 1) * CH],
                                start=False, stop=(j == 1)), sPE if j == 1 else None)
                    op("tensor", lambda e, g=g, p=p: e.matmul(
                        enb_bk[p][:, :], c_ones3.ap(), csflat3[:, g * 512:(g + 1) * 512],
                        start=True, stop=True), sPE)
                    t_d = sPE.n
                    wait("scalar", sPE, t_d)
                    wait("scalar", sDVE, d_trk[p])
                    for m, emb in ((0, em0p), (1, em1p)):
                        for j in range(2):
                            op("scalar", lambda e, m=m, j=j, emb=emb, h0=h0, p=p: e.activation(
                                emb[:, j * CH:(j + 1) * CH], dbk[p][m][:, j * CH:(j + 1) * CH],
                                AF.Exp, bias=s_ctok.ap()[:, m * NH + h0 + j:m * NH + h0 + j + 1]),
                                sACT)
                    t_e = op("scalar", lambda e, p=p, enbp=enbp: e.activation(enbp, enb_bk[p][:, :], AF.Exp), sACT)
                    e_trk[p] = t_e
                    wait("vector", sACT, t_e)
                    for m, emb in ((0, em0p), (1, em1p)):
                        op("vector", lambda e, m=m, emb=emb: e.tensor_tensor(
                            emb.rearrange("p (r c) -> p r c", r=2),
                            emb.rearrange("p (r c) -> p r c", r=2),
                            pb[5][:, m * CH:(m + 1) * CH].rearrange("p (r c) -> p r c", r=1)
                            .to_broadcast([128, 2, CH]), ALU.mult), sDVE)
                    t_mn = op("vector", lambda e, base=base, enbp=enbp, mnbp=mnbp: e.tensor_tensor(
                        mnbp.rearrange("p (r c) -> p r c", r=2),
                        enbp.rearrange("p (r c) -> p r c", r=2),
                        xbc_t(17)[:, base:base + CH].rearrange("p (r c) -> p r c", r=1)
                        .to_broadcast([128, 2, CH]), ALU.mult), sDVE)
                    d_trk[p] = t_mn
                    y_info[g] = (t_mn, em0p, em1p, mnbp, h0)
                    if DEBUG and ck == 0 and g == 0:
                        wait("sync", sDVE, sDVE.n)
                        dbg(dbg_em, s_em0.ap())
                        dbg(dbg_mn, s_mnb.ap())
                        wait("tensor", sST, sST.n)
                    if DEBUG and ck == 1 and g == 0:
                        wait("sync", sDVE, sDVE.n)
                        dbg(dbg_em1, s_em0.ap())
                        dbg(dbg_ctk, s_ctok.ap())
                        wait("tensor", sST, sST.n)
                if g >= 1:
                    gp = g - 1
                    t_mnp, em0p, em1p, mnbp, h0p = y_info.pop(gp)
                    wait("tensor", sDVE, max(t_mnp, last_ystt[0]))
                    for j in range(2):
                        h = h0p + j
                        op("tensor", lambda e, j=j, h=h, em0p=em0p: e.matmul(
                            pb[3][j * 64:(j + 1) * 64, 0:CH], x4t[0][:, h * 64:(h + 1) * 64],
                            em0p[:, j * CH:(j + 1) * CH], start=True, stop=False,
                            tile_position=(0, j * 64)))
                        op("tensor", lambda e, j=j, h=h, em1p=em1p: e.matmul(
                            pb[3][j * 64:(j + 1) * 64, 0:CH], x4t[1][:, h * 64:(h + 1) * 64],
                            em1p[:, j * CH:(j + 1) * CH], start=False, stop=False,
                            tile_position=(0, j * 64)))
                        op("tensor", lambda e, j=j, h=h, mnbp=mnbp: e.matmul(
                            pb[3][j * 64:(j + 1) * 64, 0:CH], a_carry.ap()[:, h * 64:(h + 1) * 64],
                            mnbp[:, j * CH:(j + 1) * CH], start=False, stop=True,
                            tile_position=(0, j * 64)), sPE if j == 1 else None)
                    t_y = sPE.n
                    wait("vector", sPE, t_y)
                    last_ystt[0] = op("vector", lambda e, gp=gp, c=c, base=base: e.scalar_tensor_tensor(
                        ybuf(gp)[:, c * CH:(c + 1) * CH], xbc_t(gp)[:, base:base + CH],
                        c_dexp.ap()[:, gp:gp + 1], pb[3][:, 0:CH], ALU.mult, ALU.add), sDVE)

            wait("scalar", sDVE, sDVE.n)
            op("scalar", lambda e, csk=csk: e.activation(s_cd[:, :], csk[:, CH - 1:CH], AF.Exp), sACT)
            wait("vector", sACT, sACT.n)
            op("vector", lambda e: e.tensor_scalar_mul(s_cdw[:, :], ones32[:, 0:128],
                                                       s_cd.ap()[:, 0:1]), sDVE)
            hilo(s_cdw[:, :], s_cdwh[:, :], s_cdwl[:, :])
            wait("tensor", sDVE, sDVE.n)
            for slab in range(4):
                wait("tensor", sDVE, sDVE.n)
                op("tensor", lambda e, slab=slab: e.matmul(
                    pb[4][:, :], s_cdwh[:, :], c_rep.ap()[:, slab * 512:(slab + 1) * 512],
                    start=True, stop=False))
                t_cb = op("tensor", lambda e, slab=slab: e.matmul(
                    pb[4][:, :], s_cdwl[:, :], c_rep.ap()[:, slab * 512:(slab + 1) * 512],
                    start=False, stop=True), sPE)
                wait("vector", sPE, t_cb)
                op("vector", lambda e, slab=slab: e.tensor_tensor(
                    xddb, a_carry.ap()[:, slab * 512:(slab + 1) * 512], pb[4][:, :], ALU.mult))
                op("vector", lambda e, slab=slab: e.tensor_tensor(
                    a_carry.ap()[:, slab * 512:(slab + 1) * 512], xddb,
                    stg[:, slab * 512:(slab + 1) * 512], ALU.add), sDVE)

        if DEBUG and st == 0:
            wait("sync", sDVE, sDVE.n)
            dbg(dbg_y, a_y.ap()[:, 0:FCZ * ST])

        zgroups = [list(range(0, 8)), list(range(8, 16))]
        z_sl = [0, 0]
        for fos_idx in zgroups:
            fos = [(f * 128, 128) for f in fos_idx]
            t_pe = proj(fos, [pb[j] for j in range(len(fos))], scrZ_hi, scrZ_lo,
                        lambda fin: xn_hi[:, fin * ST:(fin + 1) * ST],
                        lambda fin: xn_lo[:, fin * ST:(fin + 1) * ST], NFIN)
            wait("scalar", sPE, t_pe)
            for j, f in enumerate(fos_idx):
                sl = j % 2
                wait("scalar", sDVE, z_sl[sl])
                op("scalar", lambda e, j=j, sl=sl: e.activation(
                    stg[:, sl * ST:(sl + 1) * ST], pb[j][:, :], AF.Silu), sACT)
                wait("vector", sACT, sACT.n)
                z_sl[sl] = op("vector", lambda e, f=f, sl=sl: e.tensor_tensor(
                    ybuf(f), ybuf(f), stg[:, sl * ST:(sl + 1) * ST], ALU.mult), sDVE)
            wait("tensor", sDVE, sDVE.n)

        if DEBUG and st == 0:
            wait("sync", sDVE, sDVE.n)
            dbg(dbg_t, a_y.ap()[:, 0:FCZ * ST])
        if DEBUG and st == 1:
            wait("sync", sDVE, sDVE.n)
            dbg(dbg_t1, a_y.ap()[:, 0:FCZ * ST])

        wait("scalar", sDVE, sDVE.n)
        for f in range(FCZ):
            op("scalar", lambda e, f=f: e.activation(stg[:, 0:ST], ybuf(f), AF.Square), sACT)
            t_sq = sACT.n
            wait("tensor", sACT, t_sq)
            op("tensor", lambda e, f=f: e.matmul(
                pb[6][0:1, 0:ST], onescol.ap(), stg[:, 0:ST],
                start=(f == 0), stop=(f == FCZ - 1)), sPE)
            if f < FCZ - 1:
                wait("scalar", sPE, sPE.n)
        t_ssq = sPE.n
        wait("vector", sPE, t_ssq)
        op("vector", lambda e: e.tensor_scalar_mul(s_rms0[:, :], pb[6][0:1, 0:ST], 1.0 / DSSM), sDVE)
        wait("scalar", sDVE, sDVE.n)
        t_r = op("scalar", lambda e: e.activation(s_rms1[:, :], s_rms0[:, :],
                                                  AF.Sqrt, bias=np.float32(EPS)), sACT)
        wait("vector", sACT, t_r)
        t_r2 = op("vector", lambda e: e.reciprocal(s_rms0[:, :], s_rms1[:, :]), sDVE)
        wait("tensor", sDVE, t_r2)
        op("tensor", lambda e: e.matmul(pb[6][:, 0:ST], ones32[0:1, 0:128],
                                        s_rms0[:, :], start=True, stop=True), sPE)
        t_rbc = sPE.n
        wait("vector", sPE, t_rbc)
        for f in range(FCZ):
            op("vector", lambda e, f=f: e.scalar_tensor_tensor(
                ybuf(f), ybuf(f), c_normw.ap()[:, f:f + 1], pb[6][:, :], ALU.mult, ALU.mult), sDVE)
            hilo(ybuf(f), yn_hi[:, f * ST:(f + 1) * ST], yn_lo[:, f * ST:(f + 1) * ST])
        wait("tensor", sDVE, sDVE.n)

        ogroups = [list(range(0, 8))]
        for fos_idx in ogroups:
            fos = [(f * 128, 128) for f in fos_idx]
            t_pe = proj(fos, [pb[j] for j in range(len(fos))], scrO_hi, scrO_lo,
                        lambda fin: yn_hi[:, fin * ST:(fin + 1) * ST],
                        lambda fin: yn_lo[:, fin * ST:(fin + 1) * ST], FCZ)
            wait("scalar", sPE, t_pe)
            wait("vector", sPE, t_pe)
            for j, f in enumerate(fos_idx):
                if j % 2 == 0:
                    op("scalar", lambda e, j=j, f=f: e.activation(sb8(f), pb[j][:, :], AF.Copy), sACT)
                else:
                    op("vector", lambda e, j=j, f=f: e.tensor_copy(sb8(f), pb[j][:, :]), sDVE)
            wait("tensor", sACT, sACT.n)
            wait("tensor", sDVE, sDVE.n)

        if DEBUG and st == 1:
            wait("sync", sACT, sACT.n)
            dbg(dbg_s1, a_y.ap()[:, 0:FCS * ST])
        if DEBUG and st == 0:
            wait("sync", sACT, sACT.n)
            wait("gpsimd", sDVE, sDVE.n)
            wait("gpsimd", sACT, sACT.n)
            for f in range(FCZ):
                sGP.n += 16
                em("gpsimd", lambda e, f=f, s=sGP.h: e.dma_start(
                    out=dbg_yn[:, f * ST:(f + 1) * ST],
                    in_=yn_hi[:, f * ST:(f + 1) * ST]).then_inc(s, 16))
            dbg(dbg_s, a_y.ap()[:, 0:FCS * ST])

        for half in range(2):
            for fog in range(4):
                fos_idx = [half * 12 + fog * 3 + j for j in range(3)]
                lo_off = fos_idx[0] * 128
                w = 384
                wait("tensor", sACT, sACT.n)
                wait("tensor", sDVE, sDVE.n)
                wait("sync", sPE, sPE.n)
                wait("sync", sTR, sTR.n)
                pe_after2 = {}
                for fin in range(NFIN):
                    b = fin % 2
                    if fin >= 2:
                        wait("sync", sPE, pe_after2[fin - 2])
                    s_w = sWLa if b == 0 else sWLb
                    dma("sync", wbh[b][:, 0:w], scrG[fin, :, lo_off:lo_off + w], s_w)
                    dma("sync", wbh[b][:, w:2 * w], scrU[fin, :, lo_off:lo_off + w], s_w)
                    wait("tensor", s_w, s_w.n)
                    mh = xn_hi[:, fin * ST:(fin + 1) * ST]
                    ml = xn_lo[:, fin * ST:(fin + 1) * ST]
                    for j in range(3):
                        for gu in range(2):
                            o = gu * w + j * 128
                            pj = gu * 3 + j
                            em("tensor", lambda e, o=o, pj=pj, b=b, s=(fin == 0), mh=mh: e.matmul(
                                pb[pj][:, :], wbh[b][:, o:o + 128], mh, start=s, stop=False))
                            op("tensor", lambda e, o=o, pj=pj, b=b, p=(fin == NFIN - 1), ml=ml: e.matmul(
                                pb[pj][:, :], wbh[b][:, o:o + 128], ml, start=False, stop=p), sPE)
                    pe_after2[fin] = sPE.n
                t_pe = sPE.n
                wait("scalar", sPE, t_pe)
                for j in range(3):
                    fo = fog * 3 + j
                    wait("scalar", sDVE, sDVE.n)
                    op("scalar", lambda e, j=j: e.activation(xddb, pb[j][:, :], AF.Silu,
                                                             scale=float(QSC)), sACT)
                    t_sg2 = sACT.n
                    wait("vector", sACT, t_sg2)
                    op("vector", lambda e, j=j: e.scalar_tensor_tensor(
                        ffn_h32, pb[3 + j][:, :], float(QSC), xddb,
                        ALU.mult, ALU.mult), sDVE)
                    t_h = sDVE.n
                    wait("scalar", sDVE, t_h)
                    th = op("scalar", lambda e, fo=fo: e.activation(
                        hh[:, fo * ST:(fo + 1) * ST], ffn_h32, AF.Copy), sACT)
                    wait("vector", sACT, th)
                    op("vector", lambda e, fo=fo: e.tensor_tensor(
                        hl[:, fo * ST:(fo + 1) * ST], ffn_h32,
                        hh[:, fo * ST:(fo + 1) * ST], ALU.subtract), sDVE)
                wait("tensor", sDVE, sDVE.n)
            wait("tensor", sACT, sACT.n)
            wait("tensor", sDVE, sDVE.n)
            fos = [(d * 128, 128) for d in range(FCS)]
            t_dn = proj(fos, [pb[j] for j in range(FCS)], scrD, None,
                        lambda fin: hh[:, fin * ST:(fin + 1) * ST],
                        lambda fin: hl[:, fin * ST:(fin + 1) * ST], 12,
                        two_pass=True, fin_base=half * 12)
            if half == 0:
                wait("scalar", sPE, t_dn)
                wait("vector", sPE, t_dn)
                for do in range(FCS):
                    if do % 2 == 0:
                        op("scalar", lambda e, do=do: e.activation(
                            gacc(do), pb[do][:, :], AF.Copy), sACT)
                    else:
                        op("vector", lambda e, do=do: e.tensor_copy(
                            gacc(do), pb[do][:, :]), sDVE)
                wait("tensor", sACT, sACT.n)
                wait("tensor", sDVE, sDVE.n)
            else:
                wait("vector", sPE, t_dn)
                for do in range(FCS):
                    op("vector", lambda e, do=do: e.tensor_tensor(
                        gacc(do), gacc(do), pb[do][:, :], ALU.add), sDVE)
                wait("tensor", sDVE, sDVE.n)
        if DEBUG and st == 1:
            wait("sync", sDVE, sDVE.n)
            wait("sync", sACT, sACT.n)
            dbg(dbg_g1, a_y.ap()[:, FCS * ST:2 * FCS * ST])

        for do in range(FCS):
            wait("scalar", sDVE, sDVE.n)
            t_g = op("scalar", lambda e, do=do: e.activation(
                xddb, gacc(do), AF.Silu, scale=float(QSC)), sACT)
            wait("vector", sACT, t_g)
            op("vector", lambda e, do=do: e.tensor_tensor(
                sb8(do), sb8(do), xddb, ALU.mult), sDVE)
            wait("scalar", sDVE, sDVE.n)

        t_sg_done = sDVE.n
        wait("sync", sPE, sPE.n)
        for tg in range(4):
            wait("sync", sDVE, sDVE.n)
            dma("sync", w_xre, x_in[128 + st * ST + tg * 128:128 + st * ST + (tg + 1) * 128, :], sIO)
            cx = settle(sIO)
            wait("tensor", sDVE, t_sg_done)
            for fq in range(2):
                wait("tensor", sDVE, sDVE.n)
                for k in range(4):
                    fc = fq * 4 + k
                    op("tensor", lambda e, fc=fc, k=k, tg=tg: e.transpose(
                        pb[7][:, k * 128:(k + 1) * 128], sb8(fc)[:, tg * 128:(tg + 1) * 128],
                        c_id.ap()), sPE if k == 3 else None)
                t_tp = sPE.n
                wait("vector", sPE, t_tp)
                em("vector", lambda e, s=sIO.ch, t=cx: e.wait_ge(s, t))
                dwait("vector", sST)
                op("vector", lambda e, fq=fq: e.tensor_tensor(
                    w_ostage[:, fq * 512:(fq + 1) * 512], pb[7][:, :],
                    w_xre[:, fq * 512:(fq + 1) * 512], ALU.add), sDVE)
            t_o = sDVE.n
            wait("sync", sDVE, t_o)
            dma("sync", out_d[st * ST + tg * 128:st * ST + (tg + 1) * 128, :], w_ostage, sST)

    wait("sync", sST, sST.n)

    with nc.Block() as block:
        @block.sync
        def _(sync):
            for f in plan["sync"]:
                f(sync)

        @block.gpsimd
        def _(gpsimd):
            for f in plan["gpsimd"]:
                f(gpsimd)

        @block.scalar
        def _(scalar):
            for f in plan["scalar"]:
                f(scalar)

        @block.vector
        def _(vector):
            for f in plan["vector"]:
                f(vector)

        @block.tensor
        def _(tensor):
            for f in plan["tensor"]:
                f(tensor)

    es.close()
    return nc


def _make_consts():
    ident = np.eye(128, dtype=np.float32)
    identbf = np.eye(128, dtype=ml_dtypes.bfloat16)
    trimlt = np.zeros((128, 2 * CH), dtype=np.float32)
    for m in range(2):
        s_glob = m * 128 + np.arange(128)[:, None]
        l = np.arange(CH)[None, :]
        trimlt[:, m * CH:(m + 1) * CH] = np.where(l < s_glob, -1e30, 0.0)
    trimlt = trimlt.astype(ml_dtypes.bfloat16)
    rep = np.zeros((NH, DSSM), dtype=np.float32)
    for h in range(NH):
        rep[h, h * PD:(h + 1) * PD] = 1.0
    rep = rep.astype(ml_dtypes.bfloat16)
    ones3 = np.ones((3, 128), dtype=ml_dtypes.bfloat16)
    return ident, identbf, trimlt, rep, ones3


def kernel(L0_attn_inputs, L0_attn_W_in, L0_attn_conv_w, L0_attn_conv_b,
           L0_attn_dt_bias, L0_attn_A_log, L0_attn_D, L0_attn_norm_w,
           L0_attn_W_out, L0_ffn_w_gate, L0_ffn_w_up, L0_ffn_w_down):
    x = np.ascontiguousarray(np.asarray(L0_attn_inputs, dtype=np.float32))
    w_in = np.ascontiguousarray(np.asarray(L0_attn_W_in, dtype=np.float32))
    w_out = np.ascontiguousarray(np.asarray(L0_attn_W_out, dtype=np.float32))
    conv_w = np.asarray(L0_attn_conv_w, dtype=np.float32)
    conv_b = np.asarray(L0_attn_conv_b, dtype=np.float32)
    dt_bias = np.asarray(L0_attn_dt_bias, dtype=np.float32).reshape(NH, 1)
    a_log = np.asarray(L0_attn_A_log, dtype=np.float32).reshape(NH, 1)
    d_par = np.asarray(L0_attn_D, dtype=np.float32)
    norm_w = np.asarray(L0_attn_norm_w, dtype=np.float32)
    wg = np.ascontiguousarray(np.asarray(L0_ffn_w_gate, dtype=np.int32))
    wu = np.ascontiguousarray(np.asarray(L0_ffn_w_up, dtype=np.int32))
    wd = np.ascontiguousarray(np.asarray(L0_ffn_w_down, dtype=np.int32))

    convdiag = np.zeros((FCA, 4, 128, 128), dtype=np.float32)
    for fc in range(FCA):
        for k in range(4):
            np.fill_diagonal(convdiag[fc, k], conv_w[fc * 128:(fc + 1) * 128, k])
    convdiag = np.ascontiguousarray(convdiag.transpose(0, 2, 1, 3).reshape(FCA, 128, 512))
    convb_fm = np.ascontiguousarray(conv_b.reshape(FCA, 128).T)
    dexp_fm = np.ascontiguousarray(np.repeat(d_par, PD).reshape(FCZ, 128).T)
    normw_fm = np.ascontiguousarray(norm_w.reshape(FCZ, 128).T)
    ident, identbf, trimlt, rep, ones3 = _make_consts()

    nc = build_program()
    in_maps = []
    for core in range(8):
        b = core // 4
        q = core % 4
        if q == 0:
            halo = np.zeros((128, DM), dtype=np.float32)
        else:
            halo = x[b, q * TCORE - 128:q * TCORE]
        xc = np.concatenate([halo, x[b, q * TCORE:(q + 1) * TCORE]], axis=0)
        sel = np.zeros((128, 4), dtype=np.float32)
        if q > 0:
            sel[:, q - 1] = 1.0
        in_maps.append(dict(
            x=np.ascontiguousarray(xc), w_in=w_in, w_out=w_out,
            wg=wg, wu=wu, wd=wd,
            convdiag=convdiag, convb_fm=convb_fm, dtbias=dt_bias, alog=a_log,
            dexp_fm=dexp_fm, normw_fm=normw_fm, ident=ident, identbf=identbf,
            trimlt=trimlt, rep=rep, ones3=ones3, sel=sel))
    res = run_bass_kernel_spmd(nc, in_maps, core_ids=list(range(8)))
    out = np.zeros((2, 8192, DM), dtype=np.float32)
    for core in range(8):
        b = core // 4
        q = core % 4
        out[b, q * TCORE:(q + 1) * TCORE] = np.asarray(res.results[core]["out"])
    return out
